# version 04
# baseline (speedup 1.0000x reference)
import math
from contextlib import ExitStack

import numpy as np

import concourse.bass as bass
import concourse.mybir as mybir
from concourse.bass_utils import run_bass_kernel_spmd

F32 = mybir.dt.float32
BF16 = mybir.dt.bfloat16
I32 = mybir.dt.int32
AX = mybir.AxisListType
ALU = mybir.AluOpType
ACT = mybir.ActivationFunctionType

C = 2048
NH = 16
NKV = 4
HD = 128
KV = NKV * HD
TQ = 2048
CTX = 256
TS = TQ + CTX
GT = 512
NG = TQ // GT
ASUB = 768
NASUB = TS // ASUB
INTER = 8192
NJ = INTER // 128

TWO_PI = 2.0 * math.pi
C1_2PI = float(np.float32(TWO_PI))
C2_2PI = float(np.float32(TWO_PI - C1_2PI))
C3_2PI = float(TWO_PI - C1_2PI - C2_2PI)
LN1E4_64 = math.log(10000.0) / 64.0
W_HI = float(np.float32(-LN1E4_64))
W_LO = float(-LN1E4_64 - W_HI)
QSCALE = 1.0 / math.sqrt(HD)
EPS = 1e-5
SWITCH_NOP = 60000

ENGS = ("tensor", "vector", "scalar", "gpsimd", "sync")


def _finish(nc, q):
    with nc.Block() as block:
        @block.tensor
        def _(E):
            for f in q["tensor"]:
                f(E)

        @block.vector
        def _(E):
            for f in q["vector"]:
                f(E)

        @block.scalar
        def _(E):
            for f in q["scalar"]:
                f(E)

        @block.gpsimd
        def _(E):
            for f in q["gpsimd"]:
                f(E)

        @block.sync
        def _(E):
            for f in q["sync"]:
                f(E)


class Sem:
    def __init__(self, nc, ctx, name):
        self.h = ctx.enter_context(nc.semaphore(name))
        self.n = 0


def build(debug=False, stop_after=None, detect_races=False):
    nc = bass.Bass("TRN2", target_bir_lowering=False,
                   detect_race_conditions=detect_races)

    xs_d = nc.dram_tensor("xs", [TS, C], F32, kind="ExternalInput").ap()
    pos_d = nc.dram_tensor("pos0", [128, 1], F32, kind="ExternalInput").ap()
    wq_d = nc.dram_tensor("wq_pre", [NH * 128, C], F32, kind="ExternalInput").ap()
    wk_d = nc.dram_tensor("wk_pre", [128, NKV * C], F32, kind="ExternalInput").ap()
    wv_d = nc.dram_tensor("wv_pre", [128, 16 * KV], F32, kind="ExternalInput").ap()
    wo_d = nc.dram_tensor("wo_pre", [4 * C, 512], F32, kind="ExternalInput").ap()
    wgu_d = nc.dram_tensor("wgu_pre", [128 * 128, C], F32, kind="ExternalInput").ap()
    wd_d = nc.dram_tensor("wd_pre", [INTER, C], F32, kind="ExternalInput").ap()
    out_d = nc.dram_tensor("out", [TQ, C], F32, kind="ExternalOutput").ap()
    wqc_d = nc.dram_tensor("wq_c", [NH * 128, C], BF16, kind="Internal").ap()
    woc_d = nc.dram_tensor("wo_c", [4 * C, 512], BF16, kind="Internal").ap()
    wguc_d = nc.dram_tensor("wgu_c", [128 * 128, C], BF16, kind="Internal").ap()
    wdc_d = nc.dram_tensor("wd_c", [INTER, C], BF16, kind="Internal").ap()
    dbg = {}
    if debug:
        for nm, shp in (("d_kf", [128, NKV * TS]), ("d_vt", [128, 18 * KV]),
                        ("d_qf", [128, NH * GT]), ("d_yf", [128, NH * GT]),
                        ("d_x1", [128, 4 * C]), ("d_xg", [128, 16 * GT]),
                        ("d_hint", [128, 2048])):
            dbg[nm] = nc.dram_tensor(nm, shp, F32, kind="ExternalOutput").ap()

    with ExitStack() as ctx:
        sb = lambda n, shp, dt: ctx.enter_context(nc.sbuf_tensor(n, shp, dt))

        Kf = sb("Kf", [128, NKV * TS], BF16)
        Vt = sb("Vt", [128, 18 * KV], BF16)
        AR = sb("AR", [128, 40960], BF16)
        hintv = lambda j: AR[:, j * 512:(j + 1) * 512]
        Qfv = lambda h: AR[:, 16384 + h * 512:16384 + (h + 1) * 512]
        Yfv = lambda h: AR[:, 24576 + h * 512:24576 + (h + 1) * 512]
        xgv = lambda ci: AR[:, 32768 + ci * 512:32768 + (ci + 1) * 512]
        xlnsub = lambda ci: AR[:, ci * ASUB:(ci + 1) * ASUB]
        wk_sb = AR[:, 12288:12288 + NKV * C]
        wv_sb = AR[:, 12288 + NKV * C:12288 + NKV * C + 16 * KV]
        ARF = sb("ARF", [128, 8192], F32)
        x1v = lambda t: ARF[:, t * C:(t + 1) * C]
        acsA = ARF[:, 0:768]
        asnA = ARF[:, 768:1536]
        xtileA = lambda i: ARF[:, 1536 + i * C:1536 + (i + 1) * C]
        wbuf = [sb(f"wb{i}", [128, 2048], BF16) for i in range(4)]
        ident = sb("ident", [128, 128], F32)
        onesb = sb("onesb", [128, 384], BF16)
        Mg = sb("Mg", [128, 384], BF16)
        M0 = sb("M0", [128, 384], BF16)
        M1 = sb("M1", [128, 384], BF16)
        pos0s = sb("pos0s", [128, 1], F32)
        val01 = sb("val01", [128, 1], F32)
        omg = sb("omg", [128, 1], F32)
        stats = sb("stats", [128, 32], F32)
        ntmp = sb("ntmp", [128, 8], F32)
        itmp = sb("itmp", [128, 768], F32)
        th2 = sb("th2", [128, 768], F32)
        wtmp = sb("wtmp", [128, 768], F32)
        ki32 = sb("ki32", [128, 768], I32)
        csQ = sb("csQ", [128, 512], F32)
        snQ = sb("snQ", [128, 512], F32)
        norm_t = sb("norm_t", [128, 2048], F32)
        rtmp = sb("rtmp", [128, 512], F32)
        atmp = sb("atmp", [128, 512], F32)
        silu_t = [sb(f"silu_t{i}", [128, 512], F32) for i in range(2)]
        Pt = [sb(f"P{i}", [128, 384], F32) for i in range(2)]
        PTs = [sb(f"PT{i}", [128, 384], BF16) for i in range(2)]
        bnb = sb("bnb", [128, 24], F32)
        dmy = sb("dmyt", [128, 1], F32)

        psA = ctx.enter_context(nc.psum_tensor("psA", [128, 2048], F32))
        psB = ctx.enter_context(nc.psum_tensor("psB", [128, 2048], F32))

        pe = Sem(nc, ctx, "pe")
        wslot = [Sem(nc, ctx, f"wsl{i}") for i in range(4)]
        wslotg = [Sem(nc, ctx, f"wsg{i}") for i in range(4)]
        wbk = [Sem(nc, ctx, f"wbk{i}") for i in range(4)]
        xslot = [Sem(nc, ctx, f"xsl{i}") for i in range(2)]
        bx = [Sem(nc, ctx, f"bx{i}") for i in range(4)]
        wkvq = Sem(nc, ctx, "wkvq")
        posq = Sem(nc, ctx, "posq")
        ve = Sem(nc, ctx, "ve")
        se = Sem(nc, ctx, "se")
        ge = Sem(nc, ctx, "ge")
        sq = Sem(nc, ctx, "sq")
        gq = Sem(nc, ctx, "gq")
        cq = Sem(nc, ctx, "cq")
        so = Sem(nc, ctx, "so")

        q = {e: [] for e in ENGS}

        def W(eng, sem, val):
            if val and val > 0:
                q[eng].append(lambda E, s=sem.h, v=val: E.wait_ge(s, v))

        own_sems = {}

        def op(eng, fn, inc=None, incn=1, dma=False):
            own = own_sems.get(eng, None) if not dma else None
            if own is not None:
                if own.n > 0:
                    q[eng].append(
                        lambda E, s=own.h, v=own.n: E.wait_ge(s, v))
                if inc is None:
                    inc, incn = own, 1
            if inc is None:
                q[eng].append(fn)
                return None
            inc.n += incn
            v = inc.n
            q[eng].append(lambda E, f=fn, s=inc.h, k=incn: f(E).then_inc(s, k))
            return v

        guards = {}

        def g_wait(eng, name):
            for s_, v in guards.get(name, []):
                W(eng, s_, v)

        def g_set(name, sem, val):
            guards[name] = [(sem, val)]

        def g_add(name, sem, val):
            guards.setdefault(name, []).append((sem, val))

        own_sems["vector"] = ve
        own_sems["scalar"] = se
        own_sems["gpsimd"] = ge


        def dbg_barrier():
            for e in ("tensor", "vector", "scalar", "sync", "gpsimd"):
                W(e, cq, cq.n)
        op("gpsimd", lambda E: E.memset(onesb[:], 1.0))
        op("gpsimd", lambda E: E.memset(norm_t[:, 0:128], 1.0))
        op("gpsimd", lambda E: E.affine_select(
            out=ident[:], in_=norm_t[:, 0:128], pattern=[[-1, 128]],
            compare_op=ALU.is_equal, fill=0.0, base=0, channel_multiplier=1))
        op("gpsimd", lambda E: E.affine_select(
            out=Mg[:, 0:128], in_=onesb[:, 0:128], pattern=[[1, 128]],
            compare_op=ALU.is_ge, fill=0.0, base=0, channel_multiplier=-1))
        op("gpsimd", lambda E: E.tensor_copy(out=Mg[:, 128:256], in_=onesb[:, 0:128]))
        op("gpsimd", lambda E: E.affine_select(
            out=Mg[:, 256:384], in_=onesb[:, 0:128], pattern=[[-1, 128]],
            compare_op=ALU.is_ge, fill=0.0, base=0, channel_multiplier=1))
        op("gpsimd", lambda E: E.iota(out=ntmp[:, 0:1], pattern=[[1, 1]], base=0,
                                      channel_multiplier=1,
                                      allow_small_or_imprecise_dtypes=True), inc=ge)
        g_pro = ge.n
        op("sync", lambda E: E.dma_start(out=pos0s[:], in_=pos_d[:]),
           inc=posq, incn=16, dma=True)

        W("vector", posq, 16)
        W("vector", ge, g_pro)
        op("vector", lambda E: E.tensor_scalar(
            out=val01[:], in0=pos0s[:], scalar1=0.0, scalar2=None, op0=ALU.is_ge))
        op("vector", lambda E: E.tensor_scalar(
            out=M1[:, 0:128], in0=Mg[:, 0:128], scalar1=val01[:], scalar2=None,
            op0=ALU.mult))
        op("vector", lambda E: E.tensor_copy(out=M1[:, 128:384], in_=Mg[:, 128:384]))
        op("vector", lambda E: E.tensor_copy(out=M0[:, 0:128], in_=M1[:, 0:128]))
        op("vector", lambda E: E.tensor_scalar(
            out=M0[:, 128:256], in0=Mg[:, 128:256], scalar1=val01[:], scalar2=None,
            op0=ALU.mult))
        op("vector", lambda E: E.tensor_copy(out=M0[:, 256:384], in_=Mg[:, 256:384]))
        op("vector", lambda E: E.tensor_scalar(
            out=ntmp[64:128, 0:1], in0=ntmp[64:128, 0:1], scalar1=-64.0,
            scalar2=None, op0=ALU.add))
        op("vector", lambda E: E.tensor_scalar(
            out=ntmp[:, 1:2], in0=ntmp[:, 0:1], scalar1=W_HI, scalar2=None,
            op0=ALU.mult))
        op("vector", lambda E: E.scalar_tensor_tensor(
            out=ntmp[:, 2:3], in0=ntmp[:, 0:1], scalar=W_LO, in1=ntmp[:, 1:2],
            op0=ALU.mult, op1=ALU.add), inc=ve)
        v_omg_in = ve.n

        op("scalar", lambda E: E.activation(out=dmy[:], in_=dmy[:],
                                            func=ACT.Exp, scale=0.0))
        s_omega = None

        tab_guard = [0]

        def theta_tables(nrows, rowbase, cs_ap, sn_ap, scale, s_wait=0,
                         omega_wait=None):
            n = nrows
            W("gpsimd", se, tab_guard[0])
            op("gpsimd", lambda E, n=n: E.iota(
                out=itmp[:, 0:n], pattern=[[1, n]], base=0, channel_multiplier=0,
                allow_small_or_imprecise_dtypes=True),
                inc=ge)
            W("vector", ge, ge.n)
            if omega_wait:
                W("vector", se, omega_wait)
            op("vector", lambda E, n=n, rb=float(rowbase): E.tensor_scalar(
                out=itmp[:, 0:n], in0=itmp[:, 0:n], scalar1=pos0s[:],
                scalar2=rb, op0=ALU.add, op1=ALU.add))
            op("vector", lambda E, n=n: E.tensor_scalar(
                out=itmp[:, 0:n], in0=itmp[:, 0:n], scalar1=omg[:], scalar2=None,
                op0=ALU.mult))
            op("vector", lambda E, n=n: E.tensor_scalar(
                out=th2[:, 0:n], in0=itmp[:, 0:n], scalar1=1.0 / TWO_PI,
                scalar2=None, op0=ALU.mult))
            op("vector", lambda E, n=n: E.tensor_copy(
                out=ki32[:, 0:n], in_=th2[:, 0:n]))
            op("vector", lambda E, n=n: E.tensor_copy(
                out=th2[:, 0:n], in_=ki32[:, 0:n]))
            for cc in (C1_2PI, C2_2PI, C3_2PI):
                op("vector", lambda E, n=n, cc=cc: E.scalar_tensor_tensor(
                    out=itmp[:, 0:n], in0=th2[:, 0:n], scalar=-cc,
                    in1=itmp[:, 0:n], op0=ALU.mult, op1=ALU.add))
            op("vector", lambda E, n=n: E.tensor_scalar(
                out=th2[:, 0:n], in0=itmp[:, 0:n], scalar1=math.pi,
                scalar2=None, op0=ALU.is_gt))
            op("vector", lambda E, n=n: E.tensor_scalar(
                out=wtmp[:, 0:n], in0=itmp[:, 0:n], scalar1=-math.pi,
                scalar2=None, op0=ALU.is_lt))
            op("vector", lambda E, n=n: E.tensor_tensor(
                out=th2[:, 0:n], in0=wtmp[:, 0:n], in1=th2[:, 0:n],
                op=ALU.subtract))
            op("vector", lambda E, n=n: E.scalar_tensor_tensor(
                out=itmp[:, 0:n], in0=th2[:, 0:n], scalar=TWO_PI,
                in1=itmp[:, 0:n], op0=ALU.mult, op1=ALU.add))
            op("vector", lambda E, n=n: E.tensor_scalar(
                out=wtmp[:, 0:n], in0=itmp[:, 0:n], scalar1=math.pi / 2,
                scalar2=None, op0=ALU.add))
            op("vector", lambda E, n=n: E.tensor_scalar(
                out=th2[:, 0:n], in0=wtmp[:, 0:n], scalar1=math.pi,
                scalar2=None, op0=ALU.is_gt))
            op("vector", lambda E, n=n: E.scalar_tensor_tensor(
                out=th2[:, 0:n], in0=th2[:, 0:n], scalar=-TWO_PI,
                in1=wtmp[:, 0:n], op0=ALU.mult, op1=ALU.add), inc=ve)
            v_th = ve.n
            W("scalar", ve, v_th)
            if s_wait:
                W("scalar", se, s_wait)
            op("scalar", lambda E, n=n: E.activation(
                out=sn_ap[64:128, 0:n], in_=itmp[64:128, 0:n], func=ACT.Sin))
            op("scalar", lambda E, n=n: E.activation(
                out=sn_ap[0:64, 0:n], in_=itmp[0:64, 0:n], func=ACT.Sin,
                scale=-1.0))
            op("scalar", lambda E, n=n: E.activation(
                out=cs_ap[0:64, 0:n], in_=th2[0:64, 0:n], func=ACT.Sin))
            op("scalar", lambda E, n=n: E.activation(
                out=cs_ap[64:128, 0:n], in_=th2[64:128, 0:n], func=ACT.Sin),
                inc=se)
            s_sin = se.n
            tab_guard[0] = s_sin
            v_sc = 0
            if scale != 1.0:
                W("vector", se, s_sin)
                op("vector", lambda E, n=n: E.tensor_scalar(
                    out=cs_ap[:, 0:n], in0=cs_ap[:, 0:n], scalar1=scale,
                    scalar2=None, op0=ALU.mult))
                op("vector", lambda E, n=n: E.tensor_scalar(
                    out=sn_ap[:, 0:n], in0=sn_ap[:, 0:n], scalar1=scale,
                    scalar2=None, op0=ALU.mult), inc=ve)
                v_sc = ve.n
            return s_sin, v_sc

        def emit_ln(xin_ap, waits=()):
            m_ = stats[:, 0:1]
            v_ = stats[:, 1:2]
            r_ = stats[:, 2:3]
            mr = stats[:, 3:4]
            for s_, v in waits:
                W("vector", s_, v)
            for i in range(4):
                op("vector", lambda E, i=i, x=xin_ap: E.bn_stats(
                    out=bnb[:, i * 6:(i + 1) * 6], in_=x[:, i * 512:(i + 1) * 512]),
                    inc=ve)
            W("vector", ve, ve.n)
            op("vector", lambda E: E.bn_aggr(out=stats[:, 0:2], in_=bnb[:]),
               inc=ve)
            W("vector", ve, ve.n)
            op("vector", lambda E: E.tensor_scalar(
                out=v_, in0=v_, scalar1=EPS, scalar2=None, op0=ALU.add))
            op("vector", lambda E: E.tensor_scalar(
                out=ntmp[:, 4:5], in0=v_, scalar1=1.0, scalar2=None, op0=ALU.max))
            op("vector", lambda E: E.reciprocal(out=r_, in_=ntmp[:, 4:5]), inc=ve)
            W("vector", ve, ve.n)
            for _ in range(7):
                op("vector", lambda E: E.tensor_tensor(
                    out=ntmp[:, 5:6], in0=r_, in1=r_, op=ALU.mult))
                op("vector", lambda E: E.tensor_tensor(
                    out=ntmp[:, 5:6], in0=ntmp[:, 5:6], in1=v_, op=ALU.mult))
                op("vector", lambda E: E.tensor_scalar(
                    out=ntmp[:, 5:6], in0=ntmp[:, 5:6], scalar1=-0.5, scalar2=1.5,
                    op0=ALU.mult, op1=ALU.add))
                op("vector", lambda E: E.tensor_tensor(
                    out=r_, in0=r_, in1=ntmp[:, 5:6], op=ALU.mult))
            op("vector", lambda E: E.tensor_tensor(
                out=mr, in0=m_, in1=r_, op=ALU.mult))
            op("vector", lambda E: E.tensor_scalar(
                out=mr, in0=mr, scalar1=-1.0, scalar2=None, op0=ALU.mult), inc=ve)
            return ve.n, r_, mr

        def emit_norm_transpose(xin_ap, dst_fn, waits=()):
            vln, r_, mr = emit_ln(xin_ap, waits)
            W("vector", ve, vln)
            op("vector", lambda E, x=xin_ap, r=r_, m=mr: E.tensor_scalar(
                out=norm_t[:], in0=x, scalar1=r, scalar2=m,
                op0=ALU.mult, op1=ALU.add), inc=ve)
            vn = ve.n
            for half in range(2):
                W("tensor", ve, vn)
                g_wait("tensor", "psB2")
                g_wait("tensor", "psB3")
                for k in range(8):
                    ci = half * 8 + k
                    op("tensor", lambda E, ci=ci, k=k: E.transpose(
                        psB[:, 1024 + k * 128:1024 + (k + 1) * 128],
                        norm_t[:, ci * 128:(ci + 1) * 128], ident[:]),
                        inc=(pe if k == 7 else None))
                W("vector", pe, pe.n)
                for k in range(8):
                    ci = half * 8 + k
                    op("vector", lambda E, ci=ci, k=k: E.tensor_copy(
                        out=dst_fn(ci),
                        in_=psB[:, 1024 + k * 128:1024 + (k + 1) * 128]),
                        inc=(ve if k == 7 else None))
                g_set("psB2", ve, ve.n)
                g_set("psB3", ve, ve.n)
            return ve.n

        slot_guard = [[] for _ in range(4)]
        ws_i = [0]

        def wload(g, src_f32, cache_ap, cols=2048):
            sl = ws_i[0] % 4
            ws_i[0] += 1
            bufv = wbuf[sl][:, 0:cols]
            if g == 0:
                for s_, v in slot_guard[sl]:
                    W("gpsimd", s_, v)
                lv = op("gpsimd", lambda E, b=bufv, s=src_f32: E.dma_start(
                    out=b, in_=s), inc=wslotg[sl], incn=16, dma=True)
                W("sync", wslotg[sl], lv)
                wv = op("sync", lambda E, b=bufv, d=cache_ap: E.dma_start(
                    out=d, in_=b), inc=wbk[sl], incn=16, dma=True)
                slot_guard[sl] = [(wbk[sl], wv)]
                return sl, bufv, wslotg[sl], lv
            else:
                for s_, v in slot_guard[sl]:
                    W("sync", s_, v)
                lv = op("sync", lambda E, b=bufv, s=cache_ap: E.dma_start(
                    out=b, in_=s), inc=wslot[sl], incn=16, dma=True)
                slot_guard[sl] = []
            return sl, bufv, wslot[sl], lv

        def wdone(sl, sem, val):
            slot_guard[sl].append((sem, val))

        op("gpsimd", lambda E: E.dma_start(
            out=wk_sb, in_=wk_d.rearrange("p (k c) -> p k c", k=NKV)),
            inc=wkvq, incn=16, dma=True)
        op("gpsimd", lambda E: E.dma_start(out=wv_sb, in_=wv_d[:]),
           inc=wkvq, incn=16, dma=True)

        xc_guard = [[], []]
        for sub in range(NASUB):
            base = sub * ASUB
            for t in range(ASUB // 128):
                slot = t % 2
                row0 = base + t * 128
                for s_, v in xc_guard[slot]:
                    W("sync", s_, v)
                lv = op("sync", lambda E, s=slot, r=row0: E.dma_start(
                    out=xtileA(s), in_=xs_d[r:r + 128, :]),
                    inc=xslot[slot], incn=16, dma=True)
                wl = [(xslot[slot], lv)]
                if sub == 0 and t == 0:
                    wl.append((ge, g_pro))
                vdone = emit_norm_transpose(
                    xtileA(slot),
                    lambda ci, t=t: xlnsub(ci)[:, t * 128:(t + 1) * 128],
                    waits=tuple(wl))
                xc_guard[slot] = [(ve, vdone)]
                if sub == 0 and t == 3:
                    W("scalar", ve, v_omg_in)
                    W("scalar", ve, vdone)
                    op("scalar", lambda E: E.activation(
                        out=omg[:], in_=ntmp[:, 2:3], func=ACT.Exp), inc=se)
                    s_omega = se.n
                    op("scalar", lambda E: E.activation(
                        out=dmy[:], in_=dmy[:], func=ACT.Sin, scale=0.0))
            a_ln_v = ve.n

            W("tensor", wkvq, 32)
            W("tensor", ve, a_ln_v)
            for t in range(ASUB // 128):
                bank = ["psA0", "psA1"][t % 2]
                b = t % 2
                g_wait("tensor", bank)
                for ci in range(16):
                    op("tensor", lambda E, b=b, ci=ci, t=t: E.matmul(
                        psA[:, b * 512:(b + 1) * 512],
                        xlnsub(ci)[:, t * 128:(t + 1) * 128],
                        wv_sb[:, ci * 512:(ci + 1) * 512],
                        start=(ci == 0), stop=(ci == 15)),
                        inc=(pe if ci == 15 else None))
                W("vector", pe, pe.n)
                kt = base // 128 + t
                op("vector", lambda E, b=b, kt=kt: E.tensor_copy(
                    out=Vt[:, kt * KV:(kt + 1) * KV],
                    in_=psA[:, b * 512:(b + 1) * 512]), inc=ve)
                g_set(bank, ve, ve.n)

            g_wait("vector", "atab")
            a_tab_s, _ = theta_tables(ASUB, base, acsA, asnA, 1.0,
                                      omega_wait=s_omega)

            kix = 0
            for nch in range(2):
                for kvh in range(NKV):
                    bank = ["psB0", "psB1"][kix % 2]
                    b = kix % 2
                    kix += 1
                    g_wait("tensor", bank)
                    for ci in range(16):
                        op("tensor", lambda E, b=b, ci=ci, kvh=kvh, nch=nch:
                            E.matmul(
                                psB[:, b * 512:b * 512 + 384],
                                wk_sb[:, kvh * C + ci * 128:kvh * C + (ci + 1) * 128],
                                xlnsub(ci)[:, nch * 384:(nch + 1) * 384],
                                start=(ci == 0), stop=(ci == 15)),
                            inc=(pe if ci == 15 else None))
                    W("vector", pe, pe.n)
                    if kix == 1:
                        W("vector", se, a_tab_s)
                    cs_ = acsA[:, nch * 384:(nch + 1) * 384]
                    sn_ = asnA[:, nch * 384:(nch + 1) * 384]
                    psk = psB[:, b * 512:b * 512 + 384]
                    dst = Kf[:, kvh * TS + base + nch * 384:
                             kvh * TS + base + (nch + 1) * 384]
                    op("vector", lambda E, p=psk: E.tensor_copy(
                        out=rtmp[0:64, 0:384], in_=p[64:128, :]))
                    op("vector", lambda E, p=psk: E.tensor_copy(
                        out=rtmp[64:128, 0:384], in_=p[0:64, :]))
                    op("vector", lambda E, p=psk, c=cs_: E.tensor_tensor(
                        out=atmp[:, 0:384], in0=p, in1=c, op=ALU.mult))
                    op("vector", lambda E, s=sn_: E.tensor_tensor(
                        out=rtmp[:, 0:384], in0=rtmp[:, 0:384], in1=s,
                        op=ALU.mult))
                    op("vector", lambda E, d=dst: E.tensor_tensor(
                        out=d, in0=atmp[:, 0:384], in1=rtmp[:, 0:384],
                        op=ALU.add), inc=ve)
                    g_set(bank, ve, ve.n)
                    g_add("atab", ve, ve.n)
        a_done_v = ve.n

        if debug:
            W("gpsimd", ve, a_done_v)
            op("gpsimd", lambda E: E.dma_start(out=dbg["d_kf"], in_=Kf[:]),
               inc=cq, incn=16, dma=True)
            op("gpsimd", lambda E: E.dma_start(out=dbg["d_vt"], in_=Vt[:]),
               inc=cq, incn=16, dma=True)
            dbg_barrier()

        if stop_after == "A":
            if debug:
                W("gpsimd", ve, a_done_v)
                op("gpsimd", lambda E: E.dma_start(
                    out=dbg["d_qf"], in_=AR[:, 0:NH * GT]), inc=cq, incn=16, dma=True)
                op("gpsimd", lambda E: E.dma_start(
                    out=dbg["d_x1"], in_=ARF[:]), inc=cq, incn=16, dma=True)
                op("gpsimd", lambda E: E.dma_start(
                    out=dbg["d_xg"], in_=AR[:, 12288:12288 + NH * GT]),
                    inc=cq, incn=16, dma=True)
            W("sync", cq, cq.n)
            _finish(nc, q)
            return nc

        s_last_silu = 0
        mult_done_prev = [0, 0]
        exp_guard = [0, 0]
        p_free = [0, 0]
        pt_free = [0, 0]
        yf_sread = {}
        for g in range(NG):
            q0 = CTX + g * GT

            b1_xld = []
            for t in range(4):
                if g == 0:
                    W("sync", ve, a_done_v)
                g_wait("sync", f"x1_{t}")
                lv = op("sync", lambda E, t=t, r=q0 + t * 128: E.dma_start(
                    out=x1v(t), in_=xs_d[r:r + 128, :]),
                    inc=bx[t], incn=16, dma=True)
                b1_xld.append(lv)

            g_wait("vector", "qtab")
            b1_tab_s, b1_tab_v = theta_tables(GT, q0, csQ, snQ, QSCALE,
                                              s_wait=s_last_silu,
                                              omega_wait=s_omega)
            op("scalar", lambda E: E.activation(out=dmy[:], in_=dmy[:],
                                                func=ACT.Exp, scale=0.0))

            for t in range(4):
                emit_norm_transpose(
                    x1v(t),
                    lambda ci, t=t: xgv(ci)[:, t * 128:(t + 1) * 128],
                    waits=((bx[t], b1_xld[t]),))
            b1_xg_v = ve.n

            for h in range(NH):
                sl, bufv, ldsem, lv = wload(
                    g, wq_d[h * 128:(h + 1) * 128, :],
                    wqc_d[h * 128:(h + 1) * 128, :])
                bank = ["psA0", "psA1"][h % 2]
                b = h % 2
                W("tensor", ldsem, lv)
                if h == 0:
                    W("tensor", ve, b1_xg_v)
                g_wait("tensor", bank)
                for ci in range(16):
                    op("tensor", lambda E, b=b, ci=ci, w=bufv: E.matmul(
                        psA[:, b * 512:(b + 1) * 512],
                        w[:, ci * 128:(ci + 1) * 128],
                        xgv(ci),
                        start=(ci == 0), stop=(ci == 15)),
                        inc=(pe if ci == 15 else None))
                pq = pe.n
                wdone(sl, pe, pq)
                W("vector", pe, pq)
                if h == 0:
                    W("vector", se, b1_tab_s)
                    W("vector", ve, b1_tab_v)
                psq = psA[:, b * 512:(b + 1) * 512]
                op("vector", lambda E, p=psq: E.tensor_copy(
                    out=rtmp[0:64, :], in_=p[64:128, :]))
                op("vector", lambda E, p=psq: E.tensor_copy(
                    out=rtmp[64:128, :], in_=p[0:64, :]))
                op("vector", lambda E, p=psq: E.tensor_tensor(
                    out=atmp[:], in0=p, in1=csQ[:], op=ALU.mult))
                op("vector", lambda E: E.tensor_tensor(
                    out=rtmp[:], in0=rtmp[:], in1=snQ[:], op=ALU.mult))
                op("vector", lambda E, h=h: E.tensor_tensor(
                    out=Qfv(h), in0=atmp[:], in1=rtmp[:], op=ALU.add), inc=ve)
                g_set(bank, ve, ve.n)
            b1_q_v = ve.n
            g_set("qtab", ve, b1_q_v)

            if debug and g == 0:
                W("gpsimd", ve, b1_q_v)
                op("gpsimd", lambda E: E.dma_start(
                    out=dbg["d_qf"], in_=AR[:, 16384:16384 + NH * GT]),
                    inc=cq, incn=16, dma=True)
                op("gpsimd", lambda E: E.dma_start(
                    out=dbg["d_xg"], in_=AR[:, 32768:40960]), inc=cq, incn=16, dma=True)
                dbg_barrier()

            if stop_after == "B1" and g == 0:
                W("sync", cq, cq.n)
                _finish(nc, q)
                return nc

            NIT = 4 * NH

            def b2_item(idx):
                qt, h = idx // NH, idx % NH
                qg = 4 * g + qt
                return qt, h, qg, (M0 if qg == 0 else (M1 if qg == 1 else Mg))

            def emit_S(idx):
                qt, h, qg, mask = b2_item(idx)
                sl2 = idx % 2
                psS = psB[:, sl2 * 512:sl2 * 512 + 384]
                W("tensor", ve, b1_q_v)
                g_wait("tensor", ["psB0", "psB1"][sl2])
                W("tensor", se, exp_guard[sl2])
                op("tensor", lambda E, s=psS, h=h, qt=qt, kw=128 * qg: E.matmul(
                    s, Qfv(h)[:, qt * 128:(qt + 1) * 128],
                    Kf[:, (h // 4) * TS + kw:(h // 4) * TS + kw + 384],
                    start=True, stop=True), inc=pe)
                b2_ps[idx] = pe.n
                W("vector", pe, b2_ps[idx])
                nm = stats[:, 8 + sl2:9 + sl2]
                op("vector", lambda E, s=psS, d=nm: E.reduce_max(
                    out=d, in_=s, axis=AX.X, negate=True), inc=ve)
                b2_nm[idx] = ve.n

            def emit_soft(idx):
                qt, h, qg, mask = b2_item(idx)
                sl2 = idx % 2
                psS = psB[:, sl2 * 512:sl2 * 512 + 384]
                nm = stats[:, 8 + sl2:9 + sl2]
                W("scalar", ve, b2_nm[idx])
                W("scalar", pe, p_free[sl2])
                op("scalar", lambda E, s=psS, d=Pt[sl2], b=nm: E.activation(
                    out=d[:], in_=s, func=ACT.Exp, bias=b, scale=1.0), inc=se)
                exp_guard[sl2] = se.n
                b2_exp[idx] = se.n
                W("vector", se, b2_exp[idx])
                rs = stats[:, 12 + sl2:13 + sl2]
                ri = stats[:, 16 + sl2:17 + sl2]
                op("vector", lambda E, p=Pt[sl2], m=mask, r=rs:
                    E.scalar_tensor_tensor(
                        out=p[:], in0=p[:], scalar=1.0, in1=m[:],
                        op0=ALU.mult, op1=ALU.mult, accum_out=r))
                op("vector", lambda E, r=rs, o=ri: E.reciprocal(out=o, in_=r),
                   inc=ve)
                W("vector", ve, ve.n)
                op("vector", lambda E, p=Pt[sl2], r=ri: E.tensor_scalar(
                    out=p[:], in0=p[:], scalar1=r, scalar2=None, op0=ALU.mult),
                    inc=ve)
                b2_pf[idx] = ve.n

            def emit_tail(idx):
                qt, h, qg, mask = b2_item(idx)
                sl2 = idx % 2
                tb = 1024 + sl2 * 512
                bname = f"psB{2 + sl2}"
                W("tensor", ve, b2_pf[idx])
                g_wait("tensor", bname)
                for tix in range(3):
                    op("tensor", lambda E, p=Pt[sl2], tix=tix, tb=tb: E.transpose(
                        psB[:, tb + tix * 128:tb + (tix + 1) * 128],
                        p[:, tix * 128:(tix + 1) * 128], ident[:]),
                        inc=(pe if tix == 2 else None))
                p_tr = pe.n
                p_free[sl2] = p_tr
                W("scalar", pe, p_tr)
                W("scalar", pe, pt_free[sl2])
                op("scalar", lambda E, d=PTs[sl2], tb=tb: E.copy(
                    out=d[:], in_=psB[:, tb:tb + 384]), inc=se)
                s_pte = se.n
                g_set(bname, se, s_pte)
                yb = sl2 * 512
                W("tensor", se, s_pte)
                g_wait("tensor", ["psA0", "psA1"][sl2])
                for tix in range(3):
                    kt = qg + tix
                    op("tensor", lambda E, kt=kt, h=h, d=PTs[sl2], tix=tix, yb=yb:
                        E.matmul(
                            psA[:, yb:yb + 128],
                            Vt[:, kt * KV + (h // 4) * 128:
                               kt * KV + (h // 4) * 128 + 128],
                            d[:, tix * 128:(tix + 1) * 128],
                            start=(tix == 0), stop=(tix == 2)),
                        inc=(pe if tix == 2 else None))
                p_av = pe.n
                pt_free[sl2] = p_av
                W("scalar", pe, p_av)
                op("scalar", lambda E, h=h, qt=qt, yb=yb: E.copy(
                    out=Yfv(h)[:, qt * 128:(qt + 1) * 128],
                    in_=psA[:, yb:yb + 128]), inc=se)
                g_set(["psA0", "psA1"][sl2], se, se.n)

            b2_ps = [0] * NIT
            b2_nm = [0] * NIT
            b2_exp = [0] * NIT
            b2_pf = [0] * NIT
            emit_S(0)
            for idx in range(NIT):
                if idx + 1 < NIT:
                    emit_S(idx + 1)
                emit_soft(idx)
                if idx >= 1:
                    emit_tail(idx - 1)
            emit_tail(NIT - 1)
            b2_done_s = se.n

            if debug and g == 0:
                W("gpsimd", se, b2_done_s)
                op("gpsimd", lambda E: E.dma_start(
                    out=dbg["d_yf"], in_=AR[:, 24576:24576 + NH * GT]),
                    inc=cq, incn=16, dma=True)
                dbg_barrier()

            if stop_after == "B2" and g == 0:
                W("sync", cq, cq.n)
                _finish(nc, q)
                return nc

            op("scalar", lambda E: E.activation(out=dmy[:], in_=dmy[:],
                                                func=ACT.Silu, scale=0.0))

            for co in range(4):
                pX, pXn = (psA, "psA") if co % 2 == 0 else (psB, "psB")
                for b4 in range(4):
                    g_wait("tensor", f"{pXn}{b4}")
                if co < 2:
                    W("tensor", se, b2_done_s)
                for ci in range(16):
                    sl, bufv, ldsem, lv = wload(
                        g, wo_d[co * C + ci * 128:co * C + (ci + 1) * 128, :],
                        woc_d[co * C + ci * 128:co * C + (ci + 1) * 128, :],
                        cols=512)
                    W("tensor", ldsem, lv)
                    for tk in range(4):
                        op("tensor", lambda E, pX=pX, tk=tk, ci=ci, w=bufv:
                            E.matmul(
                                pX[:, tk * 512:(tk + 1) * 512],
                                Yfv(ci)[:, tk * 128:(tk + 1) * 128],
                                w,
                                start=(ci == 0), stop=(ci == 15)),
                            inc=(pe if tk == 3 else None))
                    wdone(sl, pe, pe.n)
                p_b3 = pe.n
                W("vector", pe, p_b3)
                for tk in range(4):
                    W("vector", bx[tk], b1_xld[tk])
                    g_wait("vector", f"x1_{tk}")
                    op("vector", lambda E, pX=pX, tk=tk, co=co: E.tensor_tensor(
                        out=x1v(tk)[:, co * 512:(co + 1) * 512],
                        in0=pX[:, tk * 512:(tk + 1) * 512],
                        in1=x1v(tk)[:, co * 512:(co + 1) * 512],
                        op=ALU.add), inc=(ve if tk == 3 else None))
                for b4 in range(4):
                    g_set(f"{pXn}{b4}", ve, ve.n)
            b3_done_v = ve.n
            for t in range(4):
                guards[f"x1_{t}"] = []

            if debug and g == 0:
                W("gpsimd", ve, b3_done_v)
                op("gpsimd", lambda E: E.dma_start(
                    out=dbg["d_x1"], in_=ARF[:]), inc=cq, incn=16, dma=True)
                dbg_barrier()

            if stop_after == "B3" and g == 0:
                W("sync", cq, cq.n)
                _finish(nc, q)
                return nc

            for t in range(4):
                emit_norm_transpose(
                    x1v(t),
                    lambda ci, t=t: xgv(ci)[:, t * 128:(t + 1) * 128],
                    waits=((ve, b3_done_v),))
            ln2_v = ve.n

            mult_done = mult_done_prev
            for j in range(NJ):
                for part in range(2):
                    jj = j if part == 0 else NJ + j
                    bname = f"psA{part * 2 + (j % 2)}"
                    b = part * 2 + (j % 2)
                    sl, bufv, ldsem, lv = wload(
                        g, wgu_d[jj * 128:(jj + 1) * 128, :],
                        wguc_d[jj * 128:(jj + 1) * 128, :])
                    W("tensor", ldsem, lv)
                    if j == 0 and part == 0:
                        W("tensor", ve, ln2_v)
                        W("tensor", se, b2_done_s)
                    g_wait("tensor", bname)
                    for ci in range(16):
                        op("tensor", lambda E, b=b, ci=ci, w=bufv: E.matmul(
                            psA[:, b * 512:(b + 1) * 512],
                            w[:, ci * 128:(ci + 1) * 128],
                            xgv(ci),
                            start=(ci == 0), stop=(ci == 15)),
                            inc=(pe if ci == 15 else None))
                    wdone(sl, pe, pe.n)
                pgu = pe.n
                gb = j % 2
                ub = 2 + (j % 2)
                W("scalar", pe, pgu)
                W("scalar", ve, mult_done[j % 2])
                op("scalar", lambda E, gb=gb, j=j: E.activation(
                    out=silu_t[j % 2][:], in_=psA[:, gb * 512:(gb + 1) * 512],
                    func=ACT.Silu), inc=se)
                s_silu = se.n
                g_set(f"psA{gb}", se, s_silu)
                W("vector", se, s_silu)
                op("vector", lambda E, ub=ub, j=j: E.tensor_tensor(
                    out=hintv(j), in0=silu_t[j % 2][:],
                    in1=psA[:, ub * 512:(ub + 1) * 512], op=ALU.mult), inc=ve)
                mult_done[j % 2] = ve.n
                g_set(f"psA{ub}", ve, ve.n)
            s_last_silu = se.n
            gu_done_v = ve.n
            mult_done_prev = mult_done

            if debug and g == 0:
                W("gpsimd", ve, gu_done_v)
                op("gpsimd", lambda E: E.dma_start(
                    out=dbg["d_hint"], in_=AR[:, 0:2048]), inc=cq, incn=16, dma=True)
                dbg_barrier()

            for tp in range(2):
                for b4 in range(4):
                    g_wait("tensor", f"psA{b4}")
                    g_wait("tensor", f"psB{b4}")
                for i in range(NJ):
                    sl, bufv, ldsem, lv = wload(
                        g, wd_d[i * 128:(i + 1) * 128, :],
                        wdc_d[i * 128:(i + 1) * 128, :])
                    W("tensor", ldsem, lv)
                    if i == 0:
                        W("tensor", ve, gu_done_v)
                    for tk in range(2):
                        tok = tp * 2 + tk
                        pX = psA if tk == 0 else psB
                        for hb in range(4):
                            op("tensor", lambda E, pX=pX, hb=hb, i=i, tok=tok,
                                w=bufv: E.matmul(
                                    pX[:, hb * 512:(hb + 1) * 512],
                                    hintv(i)[:, tok * 128:(tok + 1) * 128],
                                    w[:, hb * 512:(hb + 1) * 512],
                                    start=(i == 0), stop=(i == NJ - 1)),
                                inc=(pe if (tk == 1 and hb == 3) else None))
                    wdone(sl, pe, pe.n)
                p_down = pe.n
                W("vector", pe, p_down)
                for tk in range(2):
                    tok = tp * 2 + tk
                    pX = psA if tk == 0 else psB
                    for hb in range(4):
                        op("vector", lambda E, pX=pX, hb=hb, tok=tok:
                            E.tensor_tensor(
                                out=x1v(tok)[:, hb * 512:(hb + 1) * 512],
                                in0=pX[:, hb * 512:(hb + 1) * 512],
                                in1=x1v(tok)[:, hb * 512:(hb + 1) * 512],
                                op=ALU.add), inc=(ve if hb == 3 else None))
                    v_tile = ve.n
                    W("scalar", ve, v_tile)
                    r0 = g * GT + tok * 128
                    op("scalar", lambda E, t=tok, r=r0: E.dma_start(
                        out=out_d[r:r + 128, :], in_=x1v(t)), inc=so, incn=16, dma=True)
                    pXn = "psA" if tk == 0 else "psB"
                    for b4 in range(4):
                        g_set(f"{pXn}{b4}", ve, v_tile)
            for t4 in range(4):
                guards[f"x1_{t4}"] = [(so, so.n)]


        W("sync", so, so.n)
        if debug:
            W("sync", cq, cq.n)
        _finish(nc, q)
    return nc


def _pretile(wq, wk, wv, wo, wgu, wd):
    wq_pre = np.ascontiguousarray(
        wq.reshape(16, 128, 16, 128).transpose(2, 1, 0, 3).reshape(16 * 128, C))
    wk_pre = np.ascontiguousarray(
        wk.reshape(16, 128, 4, 128).transpose(1, 2, 0, 3).reshape(128, 4 * C))
    wv_pre = np.ascontiguousarray(
        wv.reshape(16, 128, 512).transpose(1, 0, 2).reshape(128, 16 * 512))
    wo_pre = np.ascontiguousarray(
        wo.reshape(16, 128, 4, 512).transpose(2, 0, 1, 3).reshape(4 * C, 512))
    wgu_pre = np.ascontiguousarray(
        wgu.reshape(128, 128, 16, 128).transpose(0, 3, 2, 1).reshape(128 * 128, C))
    wd_pre = np.ascontiguousarray(wd.T)
    return wq_pre, wk_pre, wv_pre, wo_pre, wgu_pre, wd_pre


def make_in_maps(x, wq, wk, wv, wo, wgu, wd):
    wq_pre, wk_pre, wv_pre, wo_pre, wgu_pre, wd_pre = _pretile(
        wq, wk, wv, wo, wgu, wd)
    in_maps = []
    for c in range(8):
        b, h = c >> 1, c & 1
        if h == 0:
            xs = np.concatenate([x[b, 0:CTX], x[b, 0:TQ]], axis=0)
            p0 = -float(CTX)
        else:
            xs = x[b, TQ - CTX:2 * TQ]
            p0 = float(TQ - CTX)
        in_maps.append({
            "xs": np.ascontiguousarray(xs),
            "pos0": np.full((128, 1), p0, np.float32),
            "wq_pre": wq_pre, "wk_pre": wk_pre, "wv_pre": wv_pre,
            "wo_pre": wo_pre, "wgu_pre": wgu_pre, "wd_pre": wd_pre,
        })
    return in_maps


def kernel(L0_attn_x, L0_attn_Wq, L0_attn_Wk, L0_attn_Wv, L0_attn_Wo,
           L0_ffn_w_gate_up, L0_ffn_w_down):
    x = np.asarray(L0_attn_x, dtype=np.float32)
    in_maps = make_in_maps(
        x, np.asarray(L0_attn_Wq, np.float32), np.asarray(L0_attn_Wk, np.float32),
        np.asarray(L0_attn_Wv, np.float32), np.asarray(L0_attn_Wo, np.float32),
        np.asarray(L0_ffn_w_gate_up, np.float32),
        np.asarray(L0_ffn_w_down, np.float32))
    nc = build(debug=False)
    res = run_bass_kernel_spmd(nc, in_maps, core_ids=list(range(8)))
    out = np.empty((4, 2 * TQ, C), np.float32)
    for c in range(8):
        b, h = c >> 1, c & 1
        out[b, h * TQ:(h + 1) * TQ] = res.results[c]["out"]
    return out


# version 06
# speedup vs baseline: 1.1578x; 1.1578x over previous
import math
from contextlib import ExitStack

import numpy as np

import concourse.bass as bass
import concourse.mybir as mybir
from concourse.bass_utils import run_bass_kernel_spmd

F32 = mybir.dt.float32
BF16 = mybir.dt.bfloat16
I32 = mybir.dt.int32
AX = mybir.AxisListType
ALU = mybir.AluOpType
ACT = mybir.ActivationFunctionType

C = 2048
NH = 16
NKV = 4
HD = 128
KV = NKV * HD
TQ = 2048
CTX = 256
TS = TQ + CTX
GT = 512
NG = TQ // GT
ASUB = 768
NASUB = TS // ASUB
INTER = 8192
NJ = INTER // 128

TWO_PI = 2.0 * math.pi
C1_2PI = float(np.float32(TWO_PI))
C2_2PI = float(np.float32(TWO_PI - C1_2PI))
C3_2PI = float(TWO_PI - C1_2PI - C2_2PI)
OMEGA32 = [float(np.float32(10000.0 ** (-p / 64.0))) for p in range(64)]
QSCALE = 1.0 / math.sqrt(HD)
EPS = 1e-5
SWITCH_NOP = 60000

ENGS = ("tensor", "vector", "scalar", "gpsimd", "sync")


def _finish(nc, q):
    with nc.Block() as block:
        @block.tensor
        def _(E):
            for f in q["tensor"]:
                f(E)

        @block.vector
        def _(E):
            for f in q["vector"]:
                f(E)

        @block.scalar
        def _(E):
            for f in q["scalar"]:
                f(E)

        @block.gpsimd
        def _(E):
            for f in q["gpsimd"]:
                f(E)

        @block.sync
        def _(E):
            for f in q["sync"]:
                f(E)


class Sem:
    def __init__(self, nc, ctx, name):
        self.h = ctx.enter_context(nc.semaphore(name))
        self.n = 0


def build(debug=False, stop_after=None, detect_races=False):
    nc = bass.Bass("TRN2", target_bir_lowering=False,
                   detect_race_conditions=detect_races)

    xs_d = nc.dram_tensor("xs", [TS, C], F32, kind="ExternalInput").ap()
    pos_d = nc.dram_tensor("pos0", [128, 1], F32, kind="ExternalInput").ap()
    wq_d = nc.dram_tensor("wq_pre", [NH * 128, C], F32, kind="ExternalInput").ap()
    wk_d = nc.dram_tensor("wk_pre", [128, NKV * C], F32, kind="ExternalInput").ap()
    wv_d = nc.dram_tensor("wv_pre", [128, 16 * KV], F32, kind="ExternalInput").ap()
    wo_d = nc.dram_tensor("wo_pre", [4 * C, 512], F32, kind="ExternalInput").ap()
    wgu_d = nc.dram_tensor("wgu_pre", [128 * 128, C], F32, kind="ExternalInput").ap()
    wd_d = nc.dram_tensor("wd_pre", [INTER, C], F32, kind="ExternalInput").ap()
    out_d = nc.dram_tensor("out", [TQ, C], F32, kind="ExternalOutput").ap()
    wqc_d = nc.dram_tensor("wq_c", [NH * 128, C], BF16, kind="Internal").ap()
    woc_d = nc.dram_tensor("wo_c", [4 * C, 512], BF16, kind="Internal").ap()
    wguc_d = nc.dram_tensor("wgu_c", [128 * 128, C], BF16, kind="Internal").ap()
    wdc_d = nc.dram_tensor("wd_c", [INTER, C], BF16, kind="Internal").ap()
    dbg = {}
    if debug:
        for nm, shp in (("d_kf", [128, NKV * TS]), ("d_vt", [128, 18 * KV]),
                        ("d_qf", [128, NH * GT]), ("d_yf", [128, NH * GT]),
                        ("d_x1", [128, 4 * C]), ("d_xg", [128, 16 * GT]),
                        ("d_hint", [128, 2048])):
            dbg[nm] = nc.dram_tensor(nm, shp, F32, kind="ExternalOutput").ap()

    with ExitStack() as ctx:
        sb = lambda n, shp, dt: ctx.enter_context(nc.sbuf_tensor(n, shp, dt))

        Kf = sb("Kf", [128, NKV * TS], BF16)
        Vt = sb("Vt", [128, 18 * KV], BF16)
        AR = sb("AR", [128, 40960], BF16)
        hintv = lambda j: AR[:, j * 512:(j + 1) * 512]
        Qfv = lambda h: AR[:, 16384 + h * 512:16384 + (h + 1) * 512]
        Yfv = lambda h: AR[:, 24576 + h * 512:24576 + (h + 1) * 512]
        xgv = lambda ci: AR[:, 32768 + ci * 512:32768 + (ci + 1) * 512]
        xlnsub = lambda ci: AR[:, ci * ASUB:(ci + 1) * ASUB]
        wk_sb = AR[:, 12288:12288 + NKV * C]
        wv_sb = AR[:, 12288 + NKV * C:12288 + NKV * C + 16 * KV]
        ARF = sb("ARF", [128, 8192], F32)
        x1v = lambda t: ARF[:, t * C:(t + 1) * C]
        acsA = ARF[:, 0:768]
        asnA = ARF[:, 768:1536]
        xtileA = lambda i: ARF[:, 1536 + i * C:1536 + (i + 1) * C]
        wbuf = [sb(f"wb{i}", [128, 2048], BF16) for i in range(4)]
        ident = sb("ident", [128, 128], F32)
        onesb = sb("onesb", [128, 384], BF16)
        Mg = sb("Mg", [128, 384], BF16)
        M0 = sb("M0", [128, 384], BF16)
        M1 = sb("M1", [128, 384], BF16)
        pos0s = sb("pos0s", [128, 1], F32)
        val01 = sb("val01", [128, 1], F32)
        omg = sb("omg", [128, 1], F32)
        stats = sb("stats", [128, 32], F32)
        ntmp = sb("ntmp", [128, 8], F32)
        itmp = sb("itmp", [128, 768], F32)
        th2 = sb("th2", [128, 768], F32)
        wtmp = sb("wtmp", [128, 768], F32)
        ki32 = sb("ki32", [128, 768], I32)
        csQ = sb("csQ", [128, 512], F32)
        snQ = sb("snQ", [128, 512], F32)
        norm_t = sb("norm_t", [128, 2048], F32)
        rtmp = sb("rtmp", [128, 512], F32)
        atmp = sb("atmp", [128, 512], F32)
        silu_t = [sb(f"silu_t{i}", [128, 512], F32) for i in range(2)]
        Pt = [sb(f"P{i}", [128, 384], F32) for i in range(2)]
        PTs = [sb(f"PT{i}", [128, 384], BF16) for i in range(2)]
        bnb = sb("bnb", [128, 24], F32)
        dmy = sb("dmyt", [128, 1], F32)

        psA = ctx.enter_context(nc.psum_tensor("psA", [128, 2048], F32))
        psB = ctx.enter_context(nc.psum_tensor("psB", [128, 2048], F32))

        pe = Sem(nc, ctx, "pe")
        wslot = [Sem(nc, ctx, f"wsl{i}") for i in range(4)]
        wslotg = [Sem(nc, ctx, f"wsg{i}") for i in range(4)]
        wbk = [Sem(nc, ctx, f"wbk{i}") for i in range(4)]
        xslot = [Sem(nc, ctx, f"xsl{i}") for i in range(2)]
        bx = [Sem(nc, ctx, f"bx{i}") for i in range(4)]
        wkvq = Sem(nc, ctx, "wkvq")
        posq = Sem(nc, ctx, "posq")
        ve = Sem(nc, ctx, "ve")
        se = Sem(nc, ctx, "se")
        ge = Sem(nc, ctx, "ge")
        sq = Sem(nc, ctx, "sq")
        gq = Sem(nc, ctx, "gq")
        cq = Sem(nc, ctx, "cq")
        so = Sem(nc, ctx, "so")

        q = {e: [] for e in ENGS}

        def W(eng, sem, val):
            if val and val > 0:
                q[eng].append(lambda E, s=sem.h, v=val: E.wait_ge(s, v))

        own_sems = {}

        def op(eng, fn, inc=None, incn=1, dma=False):
            own = own_sems.get(eng, None) if not dma else None
            if own is not None:
                if own.n > 0:
                    q[eng].append(
                        lambda E, s=own.h, v=own.n: E.wait_ge(s, v))
                if inc is None:
                    inc, incn = own, 1
            if inc is None:
                q[eng].append(fn)
                return None
            inc.n += incn
            v = inc.n
            q[eng].append(lambda E, f=fn, s=inc.h, k=incn: f(E).then_inc(s, k))
            return v

        guards = {}

        def g_wait(eng, name):
            for s_, v in guards.get(name, []):
                W(eng, s_, v)

        def g_set(name, sem, val):
            guards[name] = [(sem, val)]

        def g_add(name, sem, val):
            guards.setdefault(name, []).append((sem, val))

        own_sems["vector"] = ve
        own_sems["scalar"] = se
        own_sems["gpsimd"] = ge


        def dbg_barrier():
            for e in ("tensor", "vector", "scalar", "sync", "gpsimd"):
                W(e, cq, cq.n)
        op("gpsimd", lambda E: E.memset(onesb[:], 1.0))
        op("gpsimd", lambda E: E.memset(norm_t[:, 0:128], 1.0))
        op("gpsimd", lambda E: E.affine_select(
            out=ident[:], in_=norm_t[:, 0:128], pattern=[[-1, 128]],
            compare_op=ALU.is_equal, fill=0.0, base=0, channel_multiplier=1))
        op("gpsimd", lambda E: E.affine_select(
            out=Mg[:, 0:128], in_=onesb[:, 0:128], pattern=[[1, 128]],
            compare_op=ALU.is_ge, fill=0.0, base=0, channel_multiplier=-1))
        op("gpsimd", lambda E: E.tensor_copy(out=Mg[:, 128:256], in_=onesb[:, 0:128]))
        op("gpsimd", lambda E: E.affine_select(
            out=Mg[:, 256:384], in_=onesb[:, 0:128], pattern=[[-1, 128]],
            compare_op=ALU.is_ge, fill=0.0, base=0, channel_multiplier=1))
        op("gpsimd", lambda E: E.iota(out=ntmp[:, 0:1], pattern=[[1, 1]], base=0,
                                      channel_multiplier=1,
                                      allow_small_or_imprecise_dtypes=True), inc=ge)
        g_pro = ge.n
        op("sync", lambda E: E.dma_start(out=pos0s[:], in_=pos_d[:]),
           inc=posq, incn=16, dma=True)

        W("vector", posq, 16)
        W("vector", ge, g_pro)
        op("vector", lambda E: E.tensor_scalar(
            out=val01[:], in0=pos0s[:], scalar1=0.0, scalar2=None, op0=ALU.is_ge))
        op("vector", lambda E: E.tensor_scalar(
            out=M1[:, 0:128], in0=Mg[:, 0:128], scalar1=val01[:], scalar2=None,
            op0=ALU.mult))
        op("vector", lambda E: E.tensor_copy(out=M1[:, 128:384], in_=Mg[:, 128:384]))
        op("vector", lambda E: E.tensor_copy(out=M0[:, 0:128], in_=M1[:, 0:128]))
        op("vector", lambda E: E.tensor_scalar(
            out=M0[:, 128:256], in0=Mg[:, 128:256], scalar1=val01[:], scalar2=None,
            op0=ALU.mult))
        op("vector", lambda E: E.tensor_copy(out=M0[:, 256:384], in_=Mg[:, 256:384]))
        W("vector", ge, g_pro)
        for p in range(128):
            op("vector", lambda E, p=p: E.memset(
                norm_t[0:1, p:p + 1], OMEGA32[p % 64]))
        op("vector", lambda E: E.memset(norm_t[0:1, 128:256], 0.0), inc=ve)
        v_omrow = ve.n
        W("tensor", ve, v_omrow)
        W("tensor", ge, g_pro)
        op("tensor", lambda E: E.transpose(
            psB[:, 1024:1152], norm_t[:, 0:128], ident[:]), inc=pe)
        p_omt = pe.n
        W("vector", pe, p_omt)
        op("vector", lambda E: E.tensor_copy(out=omg[:], in_=psB[:, 1024:1025]),
           inc=ve)
        g_set("psB2", ve, ve.n)
        s_omega = ("ve", ve.n)

        tab_guard = [0]

        def theta_tables(nrows, rowbase, cs_ap, sn_ap, scale, s_wait=0,
                         omega_wait=None):
            n = nrows
            W("gpsimd", se, tab_guard[0])
            op("gpsimd", lambda E, n=n: E.iota(
                out=itmp[:, 0:n], pattern=[[1, n]], base=0, channel_multiplier=0,
                allow_small_or_imprecise_dtypes=True),
                inc=ge)
            W("vector", ge, ge.n)
            if omega_wait:
                W("vector", ve, omega_wait[1])
            op("vector", lambda E, n=n, rb=float(rowbase): E.tensor_scalar(
                out=itmp[:, 0:n], in0=itmp[:, 0:n], scalar1=pos0s[:],
                scalar2=rb, op0=ALU.add, op1=ALU.add))
            op("vector", lambda E, n=n: E.tensor_scalar(
                out=itmp[:, 0:n], in0=itmp[:, 0:n], scalar1=omg[:], scalar2=None,
                op0=ALU.mult))
            op("vector", lambda E, n=n: E.tensor_copy(
                out=ki32[:, 0:n], in_=itmp[:, 0:n].bitcast(I32)))
            op("vector", lambda E, n=n: E.tensor_copy(
                out=itmp[:, 0:n].bitcast(I32), in_=ki32[:, 0:n]))
            op("vector", lambda E, n=n: E.tensor_scalar(
                out=th2[:, 0:n], in0=itmp[:, 0:n], scalar1=1.0 / TWO_PI,
                scalar2=None, op0=ALU.mult))
            op("vector", lambda E, n=n: E.tensor_copy(
                out=ki32[:, 0:n], in_=th2[:, 0:n]))
            op("vector", lambda E, n=n: E.tensor_copy(
                out=th2[:, 0:n], in_=ki32[:, 0:n]))
            for cc in (C1_2PI, C2_2PI, C3_2PI):
                op("vector", lambda E, n=n, cc=cc: E.scalar_tensor_tensor(
                    out=itmp[:, 0:n], in0=th2[:, 0:n], scalar=-cc,
                    in1=itmp[:, 0:n], op0=ALU.mult, op1=ALU.add))
            op("vector", lambda E, n=n: E.tensor_scalar(
                out=th2[:, 0:n], in0=itmp[:, 0:n], scalar1=math.pi,
                scalar2=None, op0=ALU.is_gt))
            op("vector", lambda E, n=n: E.tensor_scalar(
                out=wtmp[:, 0:n], in0=itmp[:, 0:n], scalar1=-math.pi,
                scalar2=None, op0=ALU.is_lt))
            op("vector", lambda E, n=n: E.tensor_tensor(
                out=th2[:, 0:n], in0=wtmp[:, 0:n], in1=th2[:, 0:n],
                op=ALU.subtract))
            op("vector", lambda E, n=n: E.scalar_tensor_tensor(
                out=itmp[:, 0:n], in0=th2[:, 0:n], scalar=TWO_PI,
                in1=itmp[:, 0:n], op0=ALU.mult, op1=ALU.add))
            op("vector", lambda E, n=n: E.tensor_scalar(
                out=wtmp[:, 0:n], in0=itmp[:, 0:n], scalar1=math.pi / 2,
                scalar2=None, op0=ALU.add))
            op("vector", lambda E, n=n: E.tensor_scalar(
                out=th2[:, 0:n], in0=wtmp[:, 0:n], scalar1=math.pi,
                scalar2=None, op0=ALU.is_gt))
            op("vector", lambda E, n=n: E.scalar_tensor_tensor(
                out=th2[:, 0:n], in0=th2[:, 0:n], scalar=-TWO_PI,
                in1=wtmp[:, 0:n], op0=ALU.mult, op1=ALU.add), inc=ve)
            v_th = ve.n
            W("scalar", ve, v_th)
            if s_wait:
                W("scalar", se, s_wait)
            op("scalar", lambda E, n=n: E.activation(
                out=sn_ap[64:128, 0:n], in_=itmp[64:128, 0:n], func=ACT.Sin))
            op("scalar", lambda E, n=n: E.activation(
                out=sn_ap[0:64, 0:n], in_=itmp[0:64, 0:n], func=ACT.Sin,
                scale=-1.0))
            op("scalar", lambda E, n=n: E.activation(
                out=cs_ap[0:64, 0:n], in_=th2[0:64, 0:n], func=ACT.Sin))
            op("scalar", lambda E, n=n: E.activation(
                out=cs_ap[64:128, 0:n], in_=th2[64:128, 0:n], func=ACT.Sin),
                inc=se)
            s_sin = se.n
            tab_guard[0] = s_sin
            v_sc = 0
            if scale != 1.0:
                W("vector", se, s_sin)
                op("vector", lambda E, n=n: E.tensor_scalar(
                    out=cs_ap[:, 0:n], in0=cs_ap[:, 0:n], scalar1=scale,
                    scalar2=None, op0=ALU.mult))
                op("vector", lambda E, n=n: E.tensor_scalar(
                    out=sn_ap[:, 0:n], in0=sn_ap[:, 0:n], scalar1=scale,
                    scalar2=None, op0=ALU.mult), inc=ve)
                v_sc = ve.n
            return s_sin, v_sc

        def emit_ln(xin_ap, waits=()):
            m_ = stats[:, 0:1]
            v_ = stats[:, 1:2]
            r_ = stats[:, 2:3]
            mr = stats[:, 3:4]
            for s_, v in waits:
                W("vector", s_, v)
            for i in range(4):
                op("vector", lambda E, i=i, x=xin_ap: E.bn_stats(
                    out=bnb[:, i * 6:(i + 1) * 6], in_=x[:, i * 512:(i + 1) * 512]),
                    inc=ve)
            W("vector", ve, ve.n)
            op("vector", lambda E: E.bn_aggr(out=stats[:, 0:2], in_=bnb[:]),
               inc=ve)
            W("vector", ve, ve.n)
            op("vector", lambda E: E.tensor_scalar(
                out=v_, in0=v_, scalar1=EPS, scalar2=None, op0=ALU.add))
            op("vector", lambda E: E.tensor_scalar(
                out=ntmp[:, 4:5], in0=v_, scalar1=1.0, scalar2=None, op0=ALU.max))
            op("vector", lambda E: E.reciprocal(out=r_, in_=ntmp[:, 4:5]), inc=ve)
            W("vector", ve, ve.n)
            for _ in range(7):
                op("vector", lambda E: E.tensor_tensor(
                    out=ntmp[:, 5:6], in0=r_, in1=r_, op=ALU.mult))
                op("vector", lambda E: E.tensor_tensor(
                    out=ntmp[:, 5:6], in0=ntmp[:, 5:6], in1=v_, op=ALU.mult))
                op("vector", lambda E: E.tensor_scalar(
                    out=ntmp[:, 5:6], in0=ntmp[:, 5:6], scalar1=-0.5, scalar2=1.5,
                    op0=ALU.mult, op1=ALU.add))
                op("vector", lambda E: E.tensor_tensor(
                    out=r_, in0=r_, in1=ntmp[:, 5:6], op=ALU.mult))
            op("vector", lambda E: E.tensor_tensor(
                out=mr, in0=m_, in1=r_, op=ALU.mult))
            op("vector", lambda E: E.tensor_scalar(
                out=mr, in0=mr, scalar1=-1.0, scalar2=None, op0=ALU.mult), inc=ve)
            return ve.n, r_, mr

        def emit_norm_transpose(xin_ap, dst_fn, waits=()):
            vln, r_, mr = emit_ln(xin_ap, waits)
            W("vector", ve, vln)
            op("vector", lambda E, x=xin_ap, r=r_, m=mr: E.tensor_scalar(
                out=norm_t[:], in0=x, scalar1=r, scalar2=m,
                op0=ALU.mult, op1=ALU.add), inc=ve)
            vn = ve.n
            for half in range(2):
                W("tensor", ve, vn)
                g_wait("tensor", "psB2")
                g_wait("tensor", "psB3")
                for k in range(8):
                    ci = half * 8 + k
                    op("tensor", lambda E, ci=ci, k=k: E.transpose(
                        psB[:, 1024 + k * 128:1024 + (k + 1) * 128],
                        norm_t[:, ci * 128:(ci + 1) * 128], ident[:]),
                        inc=(pe if k == 7 else None))
                W("vector", pe, pe.n)
                for k in range(8):
                    ci = half * 8 + k
                    op("vector", lambda E, ci=ci, k=k: E.tensor_copy(
                        out=dst_fn(ci),
                        in_=psB[:, 1024 + k * 128:1024 + (k + 1) * 128]),
                        inc=(ve if k == 7 else None))
                g_set("psB2", ve, ve.n)
                g_set("psB3", ve, ve.n)
            return ve.n

        slot_guard = [[] for _ in range(4)]
        ws_i = [0]

        def wload(g, src_f32, cache_ap, cols=2048):
            sl = ws_i[0] % 4
            ws_i[0] += 1
            bufv = wbuf[sl][:, 0:cols]
            if g == 0:
                for s_, v in slot_guard[sl]:
                    W("gpsimd", s_, v)
                lv = op("gpsimd", lambda E, b=bufv, s=src_f32: E.dma_start(
                    out=b, in_=s), inc=wslotg[sl], incn=16, dma=True)
                W("sync", wslotg[sl], lv)
                wv = op("sync", lambda E, b=bufv, d=cache_ap: E.dma_start(
                    out=d, in_=b), inc=wbk[sl], incn=16, dma=True)
                slot_guard[sl] = [(wbk[sl], wv)]
                return sl, bufv, wslotg[sl], lv
            else:
                for s_, v in slot_guard[sl]:
                    W("sync", s_, v)
                lv = op("sync", lambda E, b=bufv, s=cache_ap: E.dma_start(
                    out=b, in_=s), inc=wslot[sl], incn=16, dma=True)
                slot_guard[sl] = []
            return sl, bufv, wslot[sl], lv

        def wdone(sl, sem, val):
            slot_guard[sl].append((sem, val))

        op("gpsimd", lambda E: E.dma_start(
            out=wk_sb, in_=wk_d.rearrange("p (k c) -> p k c", k=NKV)),
            inc=wkvq, incn=16, dma=True)
        op("gpsimd", lambda E: E.dma_start(out=wv_sb, in_=wv_d[:]),
           inc=wkvq, incn=16, dma=True)

        xc_guard = [[], []]
        for sub in range(NASUB):
            base = sub * ASUB
            for t in range(ASUB // 128):
                slot = t % 2
                row0 = base + t * 128
                for s_, v in xc_guard[slot]:
                    W("sync", s_, v)
                lv = op("sync", lambda E, s=slot, r=row0: E.dma_start(
                    out=xtileA(s), in_=xs_d[r:r + 128, :]),
                    inc=xslot[slot], incn=16, dma=True)
                wl = [(xslot[slot], lv)]
                if sub == 0 and t == 0:
                    wl.append((ge, g_pro))
                vdone = emit_norm_transpose(
                    xtileA(slot),
                    lambda ci, t=t: xlnsub(ci)[:, t * 128:(t + 1) * 128],
                    waits=tuple(wl))
                xc_guard[slot] = [(ve, vdone)]
                if sub == 0 and t == 3:
                    W("scalar", ve, vdone)
                    op("scalar", lambda E: E.activation(
                        out=dmy[:], in_=dmy[:], func=ACT.Sin, scale=0.0))
            a_ln_v = ve.n

            W("tensor", wkvq, 32)
            W("tensor", ve, a_ln_v)
            for t in range(ASUB // 128):
                bank = ["psA0", "psA1"][t % 2]
                b = t % 2
                g_wait("tensor", bank)
                for ci in range(16):
                    op("tensor", lambda E, b=b, ci=ci, t=t: E.matmul(
                        psA[:, b * 512:(b + 1) * 512],
                        xlnsub(ci)[:, t * 128:(t + 1) * 128],
                        wv_sb[:, ci * 512:(ci + 1) * 512],
                        start=(ci == 0), stop=(ci == 15)),
                        inc=(pe if ci == 15 else None))
                W("vector", pe, pe.n)
                kt = base // 128 + t
                op("vector", lambda E, b=b, kt=kt: E.tensor_copy(
                    out=Vt[:, kt * KV:(kt + 1) * KV],
                    in_=psA[:, b * 512:(b + 1) * 512]), inc=ve)
                g_set(bank, ve, ve.n)

            g_wait("vector", "atab")
            a_tab_s, _ = theta_tables(ASUB, base, acsA, asnA, 1.0,
                                      omega_wait=s_omega)

            kix = 0
            for nch in range(2):
                for kvh in range(NKV):
                    bank = ["psB0", "psB1"][kix % 2]
                    b = kix % 2
                    kix += 1
                    g_wait("tensor", bank)
                    for ci in range(16):
                        op("tensor", lambda E, b=b, ci=ci, kvh=kvh, nch=nch:
                            E.matmul(
                                psB[:, b * 512:b * 512 + 384],
                                wk_sb[:, kvh * C + ci * 128:kvh * C + (ci + 1) * 128],
                                xlnsub(ci)[:, nch * 384:(nch + 1) * 384],
                                start=(ci == 0), stop=(ci == 15)),
                            inc=(pe if ci == 15 else None))
                    W("vector", pe, pe.n)
                    if kix == 1:
                        W("vector", se, a_tab_s)
                    cs_ = acsA[:, nch * 384:(nch + 1) * 384]
                    sn_ = asnA[:, nch * 384:(nch + 1) * 384]
                    psk = psB[:, b * 512:b * 512 + 384]
                    dst = Kf[:, kvh * TS + base + nch * 384:
                             kvh * TS + base + (nch + 1) * 384]
                    op("vector", lambda E, p=psk: E.tensor_copy(
                        out=rtmp[0:64, 0:384], in_=p[64:128, :]))
                    op("vector", lambda E, p=psk: E.tensor_copy(
                        out=rtmp[64:128, 0:384], in_=p[0:64, :]))
                    op("vector", lambda E, p=psk, c=cs_: E.tensor_tensor(
                        out=atmp[:, 0:384], in0=p, in1=c, op=ALU.mult))
                    op("vector", lambda E, s=sn_: E.tensor_tensor(
                        out=rtmp[:, 0:384], in0=rtmp[:, 0:384], in1=s,
                        op=ALU.mult))
                    op("vector", lambda E, d=dst: E.tensor_tensor(
                        out=d, in0=atmp[:, 0:384], in1=rtmp[:, 0:384],
                        op=ALU.add), inc=ve)
                    g_set(bank, ve, ve.n)
                    g_add("atab", ve, ve.n)
        a_done_v = ve.n

        if debug:
            W("gpsimd", ve, a_done_v)
            op("gpsimd", lambda E: E.dma_start(out=dbg["d_kf"], in_=Kf[:]),
               inc=cq, incn=16, dma=True)
            op("gpsimd", lambda E: E.dma_start(out=dbg["d_vt"], in_=Vt[:]),
               inc=cq, incn=16, dma=True)
            dbg_barrier()

        if stop_after == "A":
            if debug:
                W("gpsimd", ve, a_done_v)
                op("gpsimd", lambda E: E.dma_start(
                    out=dbg["d_qf"], in_=AR[:, 0:NH * GT]), inc=cq, incn=16, dma=True)
                op("gpsimd", lambda E: E.dma_start(
                    out=dbg["d_x1"], in_=ARF[:]), inc=cq, incn=16, dma=True)
                op("gpsimd", lambda E: E.dma_start(
                    out=dbg["d_xg"], in_=AR[:, 12288:12288 + NH * GT]),
                    inc=cq, incn=16, dma=True)
            W("sync", cq, cq.n)
            _finish(nc, q)
            return nc

        s_last_silu = 0
        mult_done_prev = [0, 0]
        exp_guard = [0, 0]
        p_free = [0, 0]
        pt_free = [0, 0]
        yf_sread = {}
        for g in range(NG):
            q0 = CTX + g * GT

            b1_xld = []
            for t in range(4):
                if g == 0:
                    W("sync", ve, a_done_v)
                g_wait("sync", f"x1_{t}")
                lv = op("sync", lambda E, t=t, r=q0 + t * 128: E.dma_start(
                    out=x1v(t), in_=xs_d[r:r + 128, :]),
                    inc=bx[t], incn=16, dma=True)
                b1_xld.append(lv)

            g_wait("vector", "qtab")
            b1_tab_s, b1_tab_v = theta_tables(GT, q0, csQ, snQ, QSCALE,
                                              s_wait=s_last_silu,
                                              omega_wait=s_omega)
            op("scalar", lambda E: E.activation(out=dmy[:], in_=dmy[:],
                                                func=ACT.Exp, scale=0.0))

            for t in range(4):
                emit_norm_transpose(
                    x1v(t),
                    lambda ci, t=t: xgv(ci)[:, t * 128:(t + 1) * 128],
                    waits=((bx[t], b1_xld[t]),))
            b1_xg_v = ve.n

            for h in range(NH):
                sl, bufv, ldsem, lv = wload(
                    g, wq_d[h * 128:(h + 1) * 128, :],
                    wqc_d[h * 128:(h + 1) * 128, :])
                bank = ["psA0", "psA1"][h % 2]
                b = h % 2
                W("tensor", ldsem, lv)
                if h == 0:
                    W("tensor", ve, b1_xg_v)
                g_wait("tensor", bank)
                for ci in range(16):
                    op("tensor", lambda E, b=b, ci=ci, w=bufv: E.matmul(
                        psA[:, b * 512:(b + 1) * 512],
                        w[:, ci * 128:(ci + 1) * 128],
                        xgv(ci),
                        start=(ci == 0), stop=(ci == 15)),
                        inc=(pe if ci == 15 else None))
                pq = pe.n
                wdone(sl, pe, pq)
                W("vector", pe, pq)
                if h == 0:
                    W("vector", se, b1_tab_s)
                    W("vector", ve, b1_tab_v)
                psq = psA[:, b * 512:(b + 1) * 512]
                op("vector", lambda E, p=psq: E.tensor_copy(
                    out=rtmp[0:64, :], in_=p[64:128, :]))
                op("vector", lambda E, p=psq: E.tensor_copy(
                    out=rtmp[64:128, :], in_=p[0:64, :]))
                op("vector", lambda E, p=psq: E.tensor_tensor(
                    out=atmp[:], in0=p, in1=csQ[:], op=ALU.mult))
                op("vector", lambda E: E.tensor_tensor(
                    out=rtmp[:], in0=rtmp[:], in1=snQ[:], op=ALU.mult))
                op("vector", lambda E, h=h: E.tensor_tensor(
                    out=Qfv(h), in0=atmp[:], in1=rtmp[:], op=ALU.add), inc=ve)
                g_set(bank, ve, ve.n)
            b1_q_v = ve.n
            g_set("qtab", ve, b1_q_v)

            if debug and g == 0:
                W("gpsimd", ve, b1_q_v)
                op("gpsimd", lambda E: E.dma_start(
                    out=dbg["d_qf"], in_=AR[:, 16384:16384 + NH * GT]),
                    inc=cq, incn=16, dma=True)
                op("gpsimd", lambda E: E.dma_start(
                    out=dbg["d_xg"], in_=AR[:, 32768:40960]), inc=cq, incn=16, dma=True)
                dbg_barrier()

            if stop_after == "B1" and g == 0:
                W("sync", cq, cq.n)
                _finish(nc, q)
                return nc

            NIT = 4 * NH

            def b2_item(idx):
                qt, h = idx // NH, idx % NH
                qg = 4 * g + qt
                return qt, h, qg, (M0 if qg == 0 else (M1 if qg == 1 else Mg))

            def emit_S(idx):
                qt, h, qg, mask = b2_item(idx)
                sl2 = idx % 2
                psS = psB[:, sl2 * 512:sl2 * 512 + 384]
                W("tensor", ve, b1_q_v)
                g_wait("tensor", ["psB0", "psB1"][sl2])
                W("tensor", se, exp_guard[sl2])
                op("tensor", lambda E, s=psS, h=h, qt=qt, kw=128 * qg: E.matmul(
                    s, Qfv(h)[:, qt * 128:(qt + 1) * 128],
                    Kf[:, (h // 4) * TS + kw:(h // 4) * TS + kw + 384],
                    start=True, stop=True), inc=pe)
                b2_ps[idx] = pe.n
                W("vector", pe, b2_ps[idx])
                nm = stats[:, 8 + sl2:9 + sl2]
                op("vector", lambda E, s=psS, d=nm: E.reduce_max(
                    out=d, in_=s, axis=AX.X, negate=True), inc=ve)
                b2_nm[idx] = ve.n

            def emit_soft(idx):
                qt, h, qg, mask = b2_item(idx)
                sl2 = idx % 2
                psS = psB[:, sl2 * 512:sl2 * 512 + 384]
                nm = stats[:, 8 + sl2:9 + sl2]
                W("scalar", ve, b2_nm[idx])
                W("scalar", pe, p_free[sl2])
                op("scalar", lambda E, s=psS, d=Pt[sl2], b=nm: E.activation(
                    out=d[:], in_=s, func=ACT.Exp, bias=b, scale=1.0), inc=se)
                exp_guard[sl2] = se.n
                b2_exp[idx] = se.n
                W("vector", se, b2_exp[idx])
                rs = stats[:, 12 + sl2:13 + sl2]
                ri = stats[:, 16 + sl2:17 + sl2]
                op("vector", lambda E, p=Pt[sl2], m=mask, r=rs:
                    E.scalar_tensor_tensor(
                        out=p[:], in0=p[:], scalar=1.0, in1=m[:],
                        op0=ALU.mult, op1=ALU.mult, accum_out=r))
                op("vector", lambda E, r=rs, o=ri: E.reciprocal(out=o, in_=r),
                   inc=ve)
                W("vector", ve, ve.n)
                op("vector", lambda E, p=Pt[sl2], r=ri: E.tensor_scalar(
                    out=p[:], in0=p[:], scalar1=r, scalar2=None, op0=ALU.mult),
                    inc=ve)
                b2_pf[idx] = ve.n

            def emit_tail(idx):
                qt, h, qg, mask = b2_item(idx)
                sl2 = idx % 2
                tb = 1024 + sl2 * 512
                bname = f"psB{2 + sl2}"
                W("tensor", ve, b2_pf[idx])
                g_wait("tensor", bname)
                for tix in range(3):
                    op("tensor", lambda E, p=Pt[sl2], tix=tix, tb=tb: E.transpose(
                        psB[:, tb + tix * 128:tb + (tix + 1) * 128],
                        p[:, tix * 128:(tix + 1) * 128], ident[:]),
                        inc=(pe if tix == 2 else None))
                p_tr = pe.n
                p_free[sl2] = p_tr
                W("scalar", pe, p_tr)
                W("scalar", pe, pt_free[sl2])
                op("scalar", lambda E, d=PTs[sl2], tb=tb: E.copy(
                    out=d[:], in_=psB[:, tb:tb + 384]), inc=se)
                s_pte = se.n
                g_set(bname, se, s_pte)
                yb = sl2 * 512
                W("tensor", se, s_pte)
                g_wait("tensor", ["psA0", "psA1"][sl2])
                for tix in range(3):
                    kt = qg + tix
                    op("tensor", lambda E, kt=kt, h=h, d=PTs[sl2], tix=tix, yb=yb:
                        E.matmul(
                            psA[:, yb:yb + 128],
                            Vt[:, kt * KV + (h // 4) * 128:
                               kt * KV + (h // 4) * 128 + 128],
                            d[:, tix * 128:(tix + 1) * 128],
                            start=(tix == 0), stop=(tix == 2)),
                        inc=(pe if tix == 2 else None))
                p_av = pe.n
                pt_free[sl2] = p_av
                W("scalar", pe, p_av)
                op("scalar", lambda E, h=h, qt=qt, yb=yb: E.copy(
                    out=Yfv(h)[:, qt * 128:(qt + 1) * 128],
                    in_=psA[:, yb:yb + 128]), inc=se)
                g_set(["psA0", "psA1"][sl2], se, se.n)

            b2_ps = [0] * NIT
            b2_nm = [0] * NIT
            b2_exp = [0] * NIT
            b2_pf = [0] * NIT
            emit_S(0)
            for idx in range(NIT):
                if idx + 1 < NIT:
                    emit_S(idx + 1)
                emit_soft(idx)
                if idx >= 1:
                    emit_tail(idx - 1)
            emit_tail(NIT - 1)
            b2_done_s = se.n

            if debug and g == 0:
                W("gpsimd", se, b2_done_s)
                op("gpsimd", lambda E: E.dma_start(
                    out=dbg["d_yf"], in_=AR[:, 24576:24576 + NH * GT]),
                    inc=cq, incn=16, dma=True)
                dbg_barrier()

            if stop_after == "B2" and g == 0:
                W("sync", cq, cq.n)
                _finish(nc, q)
                return nc

            op("scalar", lambda E: E.activation(out=dmy[:], in_=dmy[:],
                                                func=ACT.Silu, scale=0.0))

            for co in range(4):
                pX, pXn = (psA, "psA") if co % 2 == 0 else (psB, "psB")
                for b4 in range(4):
                    g_wait("tensor", f"{pXn}{b4}")
                if co < 2:
                    W("tensor", se, b2_done_s)
                for ci in range(16):
                    sl, bufv, ldsem, lv = wload(
                        g, wo_d[co * C + ci * 128:co * C + (ci + 1) * 128, :],
                        woc_d[co * C + ci * 128:co * C + (ci + 1) * 128, :],
                        cols=512)
                    W("tensor", ldsem, lv)
                    for tk in range(4):
                        op("tensor", lambda E, pX=pX, tk=tk, ci=ci, w=bufv:
                            E.matmul(
                                pX[:, tk * 512:(tk + 1) * 512],
                                Yfv(ci)[:, tk * 128:(tk + 1) * 128],
                                w,
                                start=(ci == 0), stop=(ci == 15)),
                            inc=(pe if tk == 3 else None))
                    wdone(sl, pe, pe.n)
                p_b3 = pe.n
                W("vector", pe, p_b3)
                for tk in range(4):
                    W("vector", bx[tk], b1_xld[tk])
                    g_wait("vector", f"x1_{tk}")
                    op("vector", lambda E, pX=pX, tk=tk, co=co: E.tensor_tensor(
                        out=x1v(tk)[:, co * 512:(co + 1) * 512],
                        in0=pX[:, tk * 512:(tk + 1) * 512],
                        in1=x1v(tk)[:, co * 512:(co + 1) * 512],
                        op=ALU.add), inc=(ve if tk == 3 else None))
                for b4 in range(4):
                    g_set(f"{pXn}{b4}", ve, ve.n)
            b3_done_v = ve.n
            for t in range(4):
                guards[f"x1_{t}"] = []

            if debug and g == 0:
                W("gpsimd", ve, b3_done_v)
                op("gpsimd", lambda E: E.dma_start(
                    out=dbg["d_x1"], in_=ARF[:]), inc=cq, incn=16, dma=True)
                dbg_barrier()

            if stop_after == "B3" and g == 0:
                W("sync", cq, cq.n)
                _finish(nc, q)
                return nc

            for t in range(4):
                emit_norm_transpose(
                    x1v(t),
                    lambda ci, t=t: xgv(ci)[:, t * 128:(t + 1) * 128],
                    waits=((ve, b3_done_v),))
            ln2_v = ve.n

            mult_done = mult_done_prev
            for j in range(NJ):
                for part in range(2):
                    jj = j if part == 0 else NJ + j
                    bname = f"psA{part * 2 + (j % 2)}"
                    b = part * 2 + (j % 2)
                    sl, bufv, ldsem, lv = wload(
                        g, wgu_d[jj * 128:(jj + 1) * 128, :],
                        wguc_d[jj * 128:(jj + 1) * 128, :])
                    W("tensor", ldsem, lv)
                    if j == 0 and part == 0:
                        W("tensor", ve, ln2_v)
                        W("tensor", se, b2_done_s)
                    g_wait("tensor", bname)
                    for ci in range(16):
                        op("tensor", lambda E, b=b, ci=ci, w=bufv: E.matmul(
                            psA[:, b * 512:(b + 1) * 512],
                            w[:, ci * 128:(ci + 1) * 128],
                            xgv(ci),
                            start=(ci == 0), stop=(ci == 15)),
                            inc=(pe if ci == 15 else None))
                    wdone(sl, pe, pe.n)
                pgu = pe.n
                gb = j % 2
                ub = 2 + (j % 2)
                W("scalar", pe, pgu)
                W("scalar", ve, mult_done[j % 2])
                op("scalar", lambda E, gb=gb, j=j: E.activation(
                    out=silu_t[j % 2][:], in_=psA[:, gb * 512:(gb + 1) * 512],
                    func=ACT.Silu), inc=se)
                s_silu = se.n
                g_set(f"psA{gb}", se, s_silu)
                W("vector", se, s_silu)
                op("vector", lambda E, ub=ub, j=j: E.tensor_tensor(
                    out=hintv(j), in0=silu_t[j % 2][:],
                    in1=psA[:, ub * 512:(ub + 1) * 512], op=ALU.mult), inc=ve)
                mult_done[j % 2] = ve.n
                g_set(f"psA{ub}", ve, ve.n)
            s_last_silu = se.n
            gu_done_v = ve.n
            mult_done_prev = mult_done

            if debug and g == 0:
                W("gpsimd", ve, gu_done_v)
                op("gpsimd", lambda E: E.dma_start(
                    out=dbg["d_hint"], in_=AR[:, 0:2048]), inc=cq, incn=16, dma=True)
                dbg_barrier()

            for tp in range(2):
                for b4 in range(4):
                    g_wait("tensor", f"psA{b4}")
                    g_wait("tensor", f"psB{b4}")
                for i in range(NJ):
                    sl, bufv, ldsem, lv = wload(
                        g, wd_d[i * 128:(i + 1) * 128, :],
                        wdc_d[i * 128:(i + 1) * 128, :])
                    W("tensor", ldsem, lv)
                    if i == 0:
                        W("tensor", ve, gu_done_v)
                    for tk in range(2):
                        tok = tp * 2 + tk
                        pX = psA if tk == 0 else psB
                        for hb in range(4):
                            op("tensor", lambda E, pX=pX, hb=hb, i=i, tok=tok,
                                w=bufv: E.matmul(
                                    pX[:, hb * 512:(hb + 1) * 512],
                                    hintv(i)[:, tok * 128:(tok + 1) * 128],
                                    w[:, hb * 512:(hb + 1) * 512],
                                    start=(i == 0), stop=(i == NJ - 1)),
                                inc=(pe if (tk == 1 and hb == 3) else None))
                    wdone(sl, pe, pe.n)
                p_down = pe.n
                W("vector", pe, p_down)
                for tk in range(2):
                    tok = tp * 2 + tk
                    pX = psA if tk == 0 else psB
                    for hb in range(4):
                        op("vector", lambda E, pX=pX, hb=hb, tok=tok:
                            E.tensor_tensor(
                                out=x1v(tok)[:, hb * 512:(hb + 1) * 512],
                                in0=pX[:, hb * 512:(hb + 1) * 512],
                                in1=x1v(tok)[:, hb * 512:(hb + 1) * 512],
                                op=ALU.add), inc=(ve if hb == 3 else None))
                    v_tile = ve.n
                    W("scalar", ve, v_tile)
                    r0 = g * GT + tok * 128
                    op("scalar", lambda E, t=tok, r=r0: E.dma_start(
                        out=out_d[r:r + 128, :], in_=x1v(t)), inc=so, incn=16, dma=True)
                    pXn = "psA" if tk == 0 else "psB"
                    for b4 in range(4):
                        g_set(f"{pXn}{b4}", ve, v_tile)
            for t4 in range(4):
                guards[f"x1_{t4}"] = [(so, so.n)]


        W("sync", so, so.n)
        if debug:
            W("sync", cq, cq.n)
        _finish(nc, q)
    return nc


def _pretile(wq, wk, wv, wo, wgu, wd):
    wq_pre = np.ascontiguousarray(
        wq.reshape(16, 128, 16, 128).transpose(2, 1, 0, 3).reshape(16 * 128, C))
    wk_pre = np.ascontiguousarray(
        wk.reshape(16, 128, 4, 128).transpose(1, 2, 0, 3).reshape(128, 4 * C))
    wv_pre = np.ascontiguousarray(
        wv.reshape(16, 128, 512).transpose(1, 0, 2).reshape(128, 16 * 512))
    wo_pre = np.ascontiguousarray(
        wo.reshape(16, 128, 4, 512).transpose(2, 0, 1, 3).reshape(4 * C, 512))
    wgu_pre = np.ascontiguousarray(
        wgu.reshape(128, 128, 16, 128).transpose(0, 3, 2, 1).reshape(128 * 128, C))
    wd_pre = np.ascontiguousarray(wd.T)
    return wq_pre, wk_pre, wv_pre, wo_pre, wgu_pre, wd_pre


def make_in_maps(x, wq, wk, wv, wo, wgu, wd):
    wq_pre, wk_pre, wv_pre, wo_pre, wgu_pre, wd_pre = _pretile(
        wq, wk, wv, wo, wgu, wd)
    in_maps = []
    for c in range(8):
        b, h = c >> 1, c & 1
        if h == 0:
            xs = np.concatenate([x[b, 0:CTX], x[b, 0:TQ]], axis=0)
            p0 = -float(CTX)
        else:
            xs = x[b, TQ - CTX:2 * TQ]
            p0 = float(TQ - CTX)
        in_maps.append({
            "xs": np.ascontiguousarray(xs),
            "pos0": np.full((128, 1), p0, np.float32),
            "wq_pre": wq_pre, "wk_pre": wk_pre, "wv_pre": wv_pre,
            "wo_pre": wo_pre, "wgu_pre": wgu_pre, "wd_pre": wd_pre,
        })
    return in_maps


def kernel(L0_attn_x, L0_attn_Wq, L0_attn_Wk, L0_attn_Wv, L0_attn_Wo,
           L0_ffn_w_gate_up, L0_ffn_w_down):
    x = np.asarray(L0_attn_x, dtype=np.float32)
    in_maps = make_in_maps(
        x, np.asarray(L0_attn_Wq, np.float32), np.asarray(L0_attn_Wk, np.float32),
        np.asarray(L0_attn_Wv, np.float32), np.asarray(L0_attn_Wo, np.float32),
        np.asarray(L0_ffn_w_gate_up, np.float32),
        np.asarray(L0_ffn_w_down, np.float32))
    nc = build(debug=False)
    res = run_bass_kernel_spmd(nc, in_maps, core_ids=list(range(8)))
    out = np.empty((4, 2 * TQ, C), np.float32)
    for c in range(8):
        b, h = c >> 1, c & 1
        out[b, h * TQ:(h + 1) * TQ] = res.results[c]["out"]
    return out


# version 07
# speedup vs baseline: 1.2332x; 1.0651x over previous
import math
from contextlib import ExitStack

import numpy as np

import concourse.bass as bass
import concourse.mybir as mybir
from concourse.bass_utils import run_bass_kernel_spmd

F32 = mybir.dt.float32
BF16 = mybir.dt.bfloat16
I32 = mybir.dt.int32
AX = mybir.AxisListType
ALU = mybir.AluOpType
ACT = mybir.ActivationFunctionType

C = 2048
NH = 16
NKV = 4
HD = 128
KV = NKV * HD
TQ = 2048
CTX = 256
TS = TQ + CTX
GT = 512
NG = TQ // GT
ASUB = 768
NASUB = TS // ASUB
INTER = 8192
NJ = INTER // 128

TWO_PI = 2.0 * math.pi
C1_2PI = float(np.float32(TWO_PI))
C2_2PI = float(np.float32(TWO_PI - C1_2PI))
C3_2PI = float(TWO_PI - C1_2PI - C2_2PI)
OMEGA32 = [float(np.float32(10000.0 ** (-p / 64.0))) for p in range(64)]
QSCALE = 1.0 / math.sqrt(HD)
EPS = 1e-5
SWITCH_NOP = 60000

ENGS = ("tensor", "vector", "scalar", "gpsimd", "sync")


def _finish(nc, q):
    with nc.Block() as block:
        @block.tensor
        def _(E):
            for f in q["tensor"]:
                f(E)

        @block.vector
        def _(E):
            for f in q["vector"]:
                f(E)

        @block.scalar
        def _(E):
            for f in q["scalar"]:
                f(E)

        @block.gpsimd
        def _(E):
            for f in q["gpsimd"]:
                f(E)

        @block.sync
        def _(E):
            for f in q["sync"]:
                f(E)


class Sem:
    def __init__(self, nc, ctx, name):
        self.h = ctx.enter_context(nc.semaphore(name))
        self.n = 0


def build(debug=False, stop_after=None, detect_races=False):
    nc = bass.Bass("TRN2", target_bir_lowering=False,
                   detect_race_conditions=detect_races)

    xs_d = nc.dram_tensor("xs", [TS, C], F32, kind="ExternalInput").ap()
    pos_d = nc.dram_tensor("pos0", [128, 1], F32, kind="ExternalInput").ap()
    wq_d = nc.dram_tensor("wq_pre", [NH * 128, C], F32, kind="ExternalInput").ap()
    wk_d = nc.dram_tensor("wk_pre", [128, NKV * C], F32, kind="ExternalInput").ap()
    wv_d = nc.dram_tensor("wv_pre", [128, 16 * KV], F32, kind="ExternalInput").ap()
    wo_d = nc.dram_tensor("wo_pre", [4 * C, 512], F32, kind="ExternalInput").ap()
    wgu_d = nc.dram_tensor("wgu_pre", [128 * 128, C], F32, kind="ExternalInput").ap()
    wd_d = nc.dram_tensor("wd_pre", [INTER, C], F32, kind="ExternalInput").ap()
    out_d = nc.dram_tensor("out", [TQ, C], F32, kind="ExternalOutput").ap()
    wqc_d = nc.dram_tensor("wq_c", [NH * 128, C], BF16, kind="Internal").ap()
    woc_d = nc.dram_tensor("wo_c", [4 * C, 512], BF16, kind="Internal").ap()
    wguc_d = nc.dram_tensor("wgu_c", [128 * 128, C], BF16, kind="Internal").ap()
    wdc_d = nc.dram_tensor("wd_c", [INTER, C], BF16, kind="Internal").ap()
    dbg = {}
    if debug:
        for nm, shp in (("d_kf", [128, NKV * TS]), ("d_vt", [128, 18 * KV]),
                        ("d_qf", [128, NH * GT]), ("d_yf", [128, NH * GT]),
                        ("d_x1", [128, 4 * C]), ("d_xg", [128, 16 * GT]),
                        ("d_hint", [128, 2048])):
            dbg[nm] = nc.dram_tensor(nm, shp, F32, kind="ExternalOutput").ap()

    with ExitStack() as ctx:
        sb = lambda n, shp, dt: ctx.enter_context(nc.sbuf_tensor(n, shp, dt))

        Kf = sb("Kf", [128, NKV * TS], BF16)
        Vt = sb("Vt", [128, 18 * KV], BF16)
        AR = sb("AR", [128, 40960], BF16)
        hintv = lambda j: AR[:, j * 512:(j + 1) * 512]
        Qfv = lambda h: AR[:, 16384 + h * 512:16384 + (h + 1) * 512]
        Yfv = lambda h: AR[:, 24576 + h * 512:24576 + (h + 1) * 512]
        xgv = lambda ci: AR[:, 32768 + ci * 512:32768 + (ci + 1) * 512]
        xlnsub = lambda ci: AR[:, ci * ASUB:(ci + 1) * ASUB]
        wk_sb = AR[:, 12288:12288 + NKV * C]
        wv_sb = AR[:, 12288 + NKV * C:12288 + NKV * C + 16 * KV]
        ARF = sb("ARF", [128, 8192], F32)
        x1v = lambda t: ARF[:, t * C:(t + 1) * C]
        acsA = ARF[:, 0:768]
        asnA = ARF[:, 768:1536]
        xtileA = lambda i: ARF[:, 1536 + i * C:1536 + (i + 1) * C]
        wbuf = [sb(f"wb{i}", [128, 2048], BF16) for i in range(4)]
        ident = sb("ident", [128, 128], F32)
        onesb = sb("onesb", [128, 384], BF16)
        Mg = sb("Mg", [128, 384], BF16)
        M0 = sb("M0", [128, 384], BF16)
        M1 = sb("M1", [128, 384], BF16)
        pos0s = sb("pos0s", [128, 1], F32)
        val01 = sb("val01", [128, 1], F32)
        omg = sb("omg", [128, 1], F32)
        stats = sb("stats", [128, 32], F32)
        ntmp = sb("ntmp", [128, 8], F32)
        itmp = sb("itmp", [128, 768], F32)
        th2 = sb("th2", [128, 768], F32)
        wtmp = sb("wtmp", [128, 768], F32)
        ki32 = sb("ki32", [128, 768], I32)
        csQ = sb("csQ", [128, 512], F32)
        snQ = sb("snQ", [128, 512], F32)
        norm_t = sb("norm_t", [128, 2048], F32)
        rtmp = sb("rtmp", [128, 512], F32)
        atmp = sb("atmp", [128, 512], F32)
        silu_t = [sb(f"silu_t{i}", [128, 512], F32) for i in range(2)]
        Pt = [sb(f"P{i}", [128, 384], F32) for i in range(2)]
        PTs = [sb(f"PT{i}", [128, 384], BF16) for i in range(2)]
        bnb = sb("bnb", [128, 24], F32)
        dmy = sb("dmyt", [128, 1], F32)

        psA = ctx.enter_context(nc.psum_tensor("psA", [128, 2048], F32))
        psB = ctx.enter_context(nc.psum_tensor("psB", [128, 2048], F32))

        pe = Sem(nc, ctx, "pe")
        wslot = [Sem(nc, ctx, f"wsl{i}") for i in range(4)]
        wslotg = [Sem(nc, ctx, f"wsg{i}") for i in range(4)]
        wbk = [Sem(nc, ctx, f"wbk{i}") for i in range(4)]
        xslot = [Sem(nc, ctx, f"xsl{i}") for i in range(3)]
        bx = [Sem(nc, ctx, f"bx{i}") for i in range(4)]
        wkvq = Sem(nc, ctx, "wkvq")
        posq = Sem(nc, ctx, "posq")
        ve = Sem(nc, ctx, "ve")
        se = Sem(nc, ctx, "se")
        ge = Sem(nc, ctx, "ge")
        sq = Sem(nc, ctx, "sq")
        gq = Sem(nc, ctx, "gq")
        cq = Sem(nc, ctx, "cq")
        so = Sem(nc, ctx, "so")

        q = {e: [] for e in ENGS}

        def W(eng, sem, val):
            if val and val > 0:
                q[eng].append(lambda E, s=sem.h, v=val: E.wait_ge(s, v))

        own_sems = {}

        def op(eng, fn, inc=None, incn=1, dma=False):
            own = own_sems.get(eng, None) if not dma else None
            if own is not None:
                if own.n > 0:
                    q[eng].append(
                        lambda E, s=own.h, v=own.n: E.wait_ge(s, v))
                if inc is None:
                    inc, incn = own, 1
            if inc is None:
                q[eng].append(fn)
                return None
            inc.n += incn
            v = inc.n
            q[eng].append(lambda E, f=fn, s=inc.h, k=incn: f(E).then_inc(s, k))
            return v

        guards = {}

        def g_wait(eng, name):
            for s_, v in guards.get(name, []):
                W(eng, s_, v)

        def g_set(name, sem, val):
            guards[name] = [(sem, val)]

        def g_add(name, sem, val):
            guards.setdefault(name, []).append((sem, val))

        own_sems["vector"] = ve
        own_sems["scalar"] = se
        own_sems["gpsimd"] = ge


        def dbg_barrier():
            for e in ("tensor", "vector", "scalar", "sync", "gpsimd"):
                W(e, cq, cq.n)
        op("gpsimd", lambda E: E.memset(onesb[:], 1.0))
        op("gpsimd", lambda E: E.memset(norm_t[:, 0:128], 1.0))
        op("gpsimd", lambda E: E.affine_select(
            out=ident[:], in_=norm_t[:, 0:128], pattern=[[-1, 128]],
            compare_op=ALU.is_equal, fill=0.0, base=0, channel_multiplier=1))
        op("gpsimd", lambda E: E.affine_select(
            out=Mg[:, 0:128], in_=onesb[:, 0:128], pattern=[[1, 128]],
            compare_op=ALU.is_ge, fill=0.0, base=0, channel_multiplier=-1))
        op("gpsimd", lambda E: E.tensor_copy(out=Mg[:, 128:256], in_=onesb[:, 0:128]))
        op("gpsimd", lambda E: E.affine_select(
            out=Mg[:, 256:384], in_=onesb[:, 0:128], pattern=[[-1, 128]],
            compare_op=ALU.is_ge, fill=0.0, base=0, channel_multiplier=1))
        op("gpsimd", lambda E: E.iota(out=ntmp[:, 0:1], pattern=[[1, 1]], base=0,
                                      channel_multiplier=1,
                                      allow_small_or_imprecise_dtypes=True), inc=ge)
        g_pro = ge.n
        op("sync", lambda E: E.dma_start(out=pos0s[:], in_=pos_d[:]),
           inc=posq, incn=16, dma=True)

        W("vector", posq, 16)
        W("vector", ge, g_pro)
        op("vector", lambda E: E.tensor_scalar(
            out=val01[:], in0=pos0s[:], scalar1=0.0, scalar2=None, op0=ALU.is_ge))
        op("vector", lambda E: E.tensor_scalar(
            out=M1[:, 0:128], in0=Mg[:, 0:128], scalar1=val01[:], scalar2=None,
            op0=ALU.mult))
        op("vector", lambda E: E.tensor_copy(out=M1[:, 128:384], in_=Mg[:, 128:384]))
        op("vector", lambda E: E.tensor_copy(out=M0[:, 0:128], in_=M1[:, 0:128]))
        op("vector", lambda E: E.tensor_scalar(
            out=M0[:, 128:256], in0=Mg[:, 128:256], scalar1=val01[:], scalar2=None,
            op0=ALU.mult))
        op("vector", lambda E: E.tensor_copy(out=M0[:, 256:384], in_=Mg[:, 256:384]))
        W("vector", ge, g_pro)
        for p in range(128):
            op("vector", lambda E, p=p: E.memset(
                norm_t[0:1, p:p + 1], OMEGA32[p % 64]))
        op("vector", lambda E: E.memset(norm_t[0:1, 128:256], 0.0), inc=ve)
        v_omrow = ve.n
        W("tensor", ve, v_omrow)
        W("tensor", ge, g_pro)
        op("tensor", lambda E: E.transpose(
            psB[:, 1024:1152], norm_t[:, 0:128], ident[:]), inc=pe)
        p_omt = pe.n
        W("vector", pe, p_omt)
        op("vector", lambda E: E.tensor_copy(out=omg[:], in_=psB[:, 1024:1025]),
           inc=ve)
        g_set("psB2", ve, ve.n)
        s_omega = ("ve", ve.n)

        tab_guard = [0]

        def theta_tables(nrows, rowbase, cs_ap, sn_ap, scale, s_wait=0,
                         omega_wait=None):
            n = nrows
            W("gpsimd", se, tab_guard[0])
            op("gpsimd", lambda E, n=n: E.iota(
                out=itmp[:, 0:n], pattern=[[1, n]], base=0, channel_multiplier=0,
                allow_small_or_imprecise_dtypes=True),
                inc=ge)
            W("vector", ge, ge.n)
            if omega_wait:
                W("vector", ve, omega_wait[1])
            op("vector", lambda E, n=n, rb=float(rowbase): E.tensor_scalar(
                out=itmp[:, 0:n], in0=itmp[:, 0:n], scalar1=pos0s[:],
                scalar2=rb, op0=ALU.add, op1=ALU.add))
            op("vector", lambda E, n=n: E.tensor_scalar(
                out=itmp[:, 0:n], in0=itmp[:, 0:n], scalar1=omg[:], scalar2=None,
                op0=ALU.mult))
            op("vector", lambda E, n=n: E.tensor_copy(
                out=ki32[:, 0:n], in_=itmp[:, 0:n].bitcast(I32)))
            op("vector", lambda E, n=n: E.tensor_copy(
                out=itmp[:, 0:n].bitcast(I32), in_=ki32[:, 0:n]))
            op("vector", lambda E, n=n: E.tensor_scalar(
                out=th2[:, 0:n], in0=itmp[:, 0:n], scalar1=1.0 / TWO_PI,
                scalar2=None, op0=ALU.mult))
            op("vector", lambda E, n=n: E.tensor_copy(
                out=ki32[:, 0:n], in_=th2[:, 0:n]))
            op("vector", lambda E, n=n: E.tensor_copy(
                out=th2[:, 0:n], in_=ki32[:, 0:n]))
            for cc in (C1_2PI, C2_2PI, C3_2PI):
                op("vector", lambda E, n=n, cc=cc: E.scalar_tensor_tensor(
                    out=itmp[:, 0:n], in0=th2[:, 0:n], scalar=-cc,
                    in1=itmp[:, 0:n], op0=ALU.mult, op1=ALU.add))
            op("vector", lambda E, n=n: E.tensor_scalar(
                out=th2[:, 0:n], in0=itmp[:, 0:n], scalar1=math.pi,
                scalar2=None, op0=ALU.is_gt))
            op("vector", lambda E, n=n: E.tensor_scalar(
                out=wtmp[:, 0:n], in0=itmp[:, 0:n], scalar1=-math.pi,
                scalar2=None, op0=ALU.is_lt))
            op("vector", lambda E, n=n: E.tensor_tensor(
                out=th2[:, 0:n], in0=wtmp[:, 0:n], in1=th2[:, 0:n],
                op=ALU.subtract))
            op("vector", lambda E, n=n: E.scalar_tensor_tensor(
                out=itmp[:, 0:n], in0=th2[:, 0:n], scalar=TWO_PI,
                in1=itmp[:, 0:n], op0=ALU.mult, op1=ALU.add))
            op("vector", lambda E, n=n: E.tensor_scalar(
                out=wtmp[:, 0:n], in0=itmp[:, 0:n], scalar1=math.pi / 2,
                scalar2=None, op0=ALU.add))
            op("vector", lambda E, n=n: E.tensor_scalar(
                out=th2[:, 0:n], in0=wtmp[:, 0:n], scalar1=math.pi,
                scalar2=None, op0=ALU.is_gt))
            op("vector", lambda E, n=n: E.scalar_tensor_tensor(
                out=th2[:, 0:n], in0=th2[:, 0:n], scalar=-TWO_PI,
                in1=wtmp[:, 0:n], op0=ALU.mult, op1=ALU.add), inc=ve)
            v_th = ve.n
            W("scalar", ve, v_th)
            if s_wait:
                W("scalar", se, s_wait)
            op("scalar", lambda E, n=n: E.activation(
                out=sn_ap[64:128, 0:n], in_=itmp[64:128, 0:n], func=ACT.Sin))
            op("scalar", lambda E, n=n: E.activation(
                out=sn_ap[0:64, 0:n], in_=itmp[0:64, 0:n], func=ACT.Sin,
                scale=-1.0))
            op("scalar", lambda E, n=n: E.activation(
                out=cs_ap[0:64, 0:n], in_=th2[0:64, 0:n], func=ACT.Sin))
            op("scalar", lambda E, n=n: E.activation(
                out=cs_ap[64:128, 0:n], in_=th2[64:128, 0:n], func=ACT.Sin),
                inc=se)
            s_sin = se.n
            tab_guard[0] = s_sin
            v_sc = 0
            if scale != 1.0:
                W("vector", se, s_sin)
                op("vector", lambda E, n=n: E.tensor_scalar(
                    out=cs_ap[:, 0:n], in0=cs_ap[:, 0:n], scalar1=scale,
                    scalar2=None, op0=ALU.mult))
                op("vector", lambda E, n=n: E.tensor_scalar(
                    out=sn_ap[:, 0:n], in0=sn_ap[:, 0:n], scalar1=scale,
                    scalar2=None, op0=ALU.mult), inc=ve)
                v_sc = ve.n
            return s_sin, v_sc

        def emit_ln_stats(xin_ap, slot, waits=()):
            for s_, v in waits:
                W("vector", s_, v)
            for i in range(4):
                op("vector", lambda E, i=i, x=xin_ap: E.bn_stats(
                    out=bnb[:, i * 6:(i + 1) * 6], in_=x[:, i * 512:(i + 1) * 512]),
                    inc=ve)
            W("vector", ve, ve.n)
            op("vector", lambda E, sl=slot: E.bn_aggr(
                out=stats[:, 4 * sl:4 * sl + 2], in_=bnb[:]), inc=ve)
            W("vector", ve, ve.n)

        def emit_newton(nslots):
            n = nslots
            m_ = stats[:, 0:4 * n:4]
            v_ = stats[:, 1:4 * n:4]
            r_ = stats[:, 2:4 * n:4]
            mr = stats[:, 3:4 * n:4]
            t_ = ntmp[:, 0:n]
            op("vector", lambda E, v_=v_: E.tensor_scalar(
                out=v_, in0=v_, scalar1=EPS, scalar2=None, op0=ALU.add))
            op("vector", lambda E, v_=v_, t_=t_: E.tensor_scalar(
                out=t_, in0=v_, scalar1=1.0, scalar2=None, op0=ALU.max))
            op("vector", lambda E, r_=r_, t_=t_: E.reciprocal(out=r_, in_=t_),
               inc=ve)
            W("vector", ve, ve.n)
            for _ in range(7):
                op("vector", lambda E, r_=r_, t_=t_: E.tensor_tensor(
                    out=t_, in0=r_, in1=r_, op=ALU.mult))
                op("vector", lambda E, v_=v_, t_=t_: E.tensor_tensor(
                    out=t_, in0=t_, in1=v_, op=ALU.mult))
                op("vector", lambda E, t_=t_: E.tensor_scalar(
                    out=t_, in0=t_, scalar1=-0.5, scalar2=1.5,
                    op0=ALU.mult, op1=ALU.add))
                op("vector", lambda E, r_=r_, t_=t_: E.tensor_tensor(
                    out=r_, in0=r_, in1=t_, op=ALU.mult))
            op("vector", lambda E, m_=m_, r_=r_, mr=mr: E.scalar_tensor_tensor(
                out=mr, in0=m_, scalar=-1.0, in1=r_, op0=ALU.mult, op1=ALU.mult),
                inc=ve)
            return ve.n

        def emit_norm_transpose(xin_ap, slot, dst_fn, waits=()):
            r_ = stats[:, 4 * slot + 2:4 * slot + 3]
            mr = stats[:, 4 * slot + 3:4 * slot + 4]
            for s_, v in waits:
                W("vector", s_, v)
            op("vector", lambda E, x=xin_ap, r=r_, m=mr: E.tensor_scalar(
                out=norm_t[:], in0=x, scalar1=r, scalar2=m,
                op0=ALU.mult, op1=ALU.add), inc=ve)
            vn = ve.n
            for half in range(2):
                W("tensor", ve, vn)
                g_wait("tensor", "psB2")
                g_wait("tensor", "psB3")
                for k in range(8):
                    ci = half * 8 + k
                    op("tensor", lambda E, ci=ci, k=k: E.transpose(
                        psB[:, 1024 + k * 128:1024 + (k + 1) * 128],
                        norm_t[:, ci * 128:(ci + 1) * 128], ident[:]),
                        inc=(pe if k == 7 else None))
                W("vector", pe, pe.n)
                for k in range(8):
                    ci = half * 8 + k
                    op("vector", lambda E, ci=ci, k=k: E.tensor_copy(
                        out=dst_fn(ci),
                        in_=psB[:, 1024 + k * 128:1024 + (k + 1) * 128]),
                        inc=(ve if k == 7 else None))
                g_set("psB2", ve, ve.n)
                g_set("psB3", ve, ve.n)
            return ve.n

        slot_guard = [[] for _ in range(4)]
        ws_i = [0]

        def wload(g, src_f32, cache_ap, cols=2048):
            sl = ws_i[0] % 4
            ws_i[0] += 1
            bufv = wbuf[sl][:, 0:cols]
            if g == 0:
                for s_, v in slot_guard[sl]:
                    W("gpsimd", s_, v)
                lv = op("gpsimd", lambda E, b=bufv, s=src_f32: E.dma_start(
                    out=b, in_=s), inc=wslotg[sl], incn=16, dma=True)
                W("sync", wslotg[sl], lv)
                wv = op("sync", lambda E, b=bufv, d=cache_ap: E.dma_start(
                    out=d, in_=b), inc=wbk[sl], incn=16, dma=True)
                slot_guard[sl] = [(wbk[sl], wv)]
                return sl, bufv, wslotg[sl], lv
            else:
                for s_, v in slot_guard[sl]:
                    W("sync", s_, v)
                lv = op("sync", lambda E, b=bufv, s=cache_ap: E.dma_start(
                    out=b, in_=s), inc=wslot[sl], incn=16, dma=True)
                slot_guard[sl] = []
            return sl, bufv, wslot[sl], lv

        def wdone(sl, sem, val):
            slot_guard[sl].append((sem, val))

        op("gpsimd", lambda E: E.dma_start(
            out=wk_sb, in_=wk_d.rearrange("p (k c) -> p k c", k=NKV)),
            inc=wkvq, incn=16, dma=True)
        op("gpsimd", lambda E: E.dma_start(out=wv_sb, in_=wv_d[:]),
           inc=wkvq, incn=16, dma=True)

        xc_guard = [[], [], []]
        for sub in range(NASUB):
            base = sub * ASUB
            for half3 in range(2):
                lvs = []
                for k3 in range(3):
                    t = half3 * 3 + k3
                    slot = k3
                    row0 = base + t * 128
                    for s_, v in xc_guard[slot]:
                        W("sync", s_, v)
                    lv = op("sync", lambda E, s=slot, r=row0: E.dma_start(
                        out=xtileA(s), in_=xs_d[r:r + 128, :]),
                        inc=xslot[slot], incn=16, dma=True)
                    lvs.append(lv)
                    wl = [(xslot[slot], lv)]
                    if sub == 0 and t == 0:
                        wl.append((ge, g_pro))
                    emit_ln_stats(xtileA(slot), slot, waits=tuple(wl))
                emit_newton(3)
                for k3 in range(3):
                    t = half3 * 3 + k3
                    vdone = emit_norm_transpose(
                        xtileA(k3), k3,
                        lambda ci, t=t: xlnsub(ci)[:, t * 128:(t + 1) * 128])
                    xc_guard[k3] = [(ve, vdone)]
                if sub == 0 and half3 == 0:
                    W("scalar", ve, ve.n)
                    op("scalar", lambda E: E.activation(
                        out=dmy[:], in_=dmy[:], func=ACT.Sin, scale=0.0))
            a_ln_v = ve.n
            a_ln_v = ve.n

            W("tensor", wkvq, 32)
            W("tensor", ve, a_ln_v)
            for t in range(ASUB // 128):
                bank = ["psA0", "psA1"][t % 2]
                b = t % 2
                g_wait("tensor", bank)
                for ci in range(16):
                    op("tensor", lambda E, b=b, ci=ci, t=t: E.matmul(
                        psA[:, b * 512:(b + 1) * 512],
                        xlnsub(ci)[:, t * 128:(t + 1) * 128],
                        wv_sb[:, ci * 512:(ci + 1) * 512],
                        start=(ci == 0), stop=(ci == 15)),
                        inc=(pe if ci == 15 else None))
                W("vector", pe, pe.n)
                kt = base // 128 + t
                op("vector", lambda E, b=b, kt=kt: E.tensor_copy(
                    out=Vt[:, kt * KV:(kt + 1) * KV],
                    in_=psA[:, b * 512:(b + 1) * 512]), inc=ve)
                g_set(bank, ve, ve.n)

            g_wait("vector", "atab")
            a_tab_s, _ = theta_tables(ASUB, base, acsA, asnA, 1.0,
                                      omega_wait=s_omega)

            kix = 0
            for nch in range(2):
                for kvh in range(NKV):
                    bank = ["psB0", "psB1"][kix % 2]
                    b = kix % 2
                    kix += 1
                    g_wait("tensor", bank)
                    for ci in range(16):
                        op("tensor", lambda E, b=b, ci=ci, kvh=kvh, nch=nch:
                            E.matmul(
                                psB[:, b * 512:b * 512 + 384],
                                wk_sb[:, kvh * C + ci * 128:kvh * C + (ci + 1) * 128],
                                xlnsub(ci)[:, nch * 384:(nch + 1) * 384],
                                start=(ci == 0), stop=(ci == 15)),
                            inc=(pe if ci == 15 else None))
                    W("vector", pe, pe.n)
                    if kix == 1:
                        W("vector", se, a_tab_s)
                    cs_ = acsA[:, nch * 384:(nch + 1) * 384]
                    sn_ = asnA[:, nch * 384:(nch + 1) * 384]
                    psk = psB[:, b * 512:b * 512 + 384]
                    dst = Kf[:, kvh * TS + base + nch * 384:
                             kvh * TS + base + (nch + 1) * 384]
                    op("vector", lambda E, p=psk: E.tensor_copy(
                        out=rtmp[0:64, 0:384], in_=p[64:128, :]))
                    op("vector", lambda E, p=psk: E.tensor_copy(
                        out=rtmp[64:128, 0:384], in_=p[0:64, :]))
                    op("vector", lambda E, p=psk, c=cs_: E.tensor_tensor(
                        out=atmp[:, 0:384], in0=p, in1=c, op=ALU.mult))
                    op("vector", lambda E, s=sn_: E.tensor_tensor(
                        out=rtmp[:, 0:384], in0=rtmp[:, 0:384], in1=s,
                        op=ALU.mult))
                    op("vector", lambda E, d=dst: E.tensor_tensor(
                        out=d, in0=atmp[:, 0:384], in1=rtmp[:, 0:384],
                        op=ALU.add), inc=ve)
                    g_set(bank, ve, ve.n)
                    g_add("atab", ve, ve.n)
        a_done_v = ve.n

        if debug:
            W("gpsimd", ve, a_done_v)
            op("gpsimd", lambda E: E.dma_start(out=dbg["d_kf"], in_=Kf[:]),
               inc=cq, incn=16, dma=True)
            op("gpsimd", lambda E: E.dma_start(out=dbg["d_vt"], in_=Vt[:]),
               inc=cq, incn=16, dma=True)
            dbg_barrier()

        if stop_after == "A":
            if debug:
                W("gpsimd", ve, a_done_v)
                op("gpsimd", lambda E: E.dma_start(
                    out=dbg["d_qf"], in_=AR[:, 0:NH * GT]), inc=cq, incn=16, dma=True)
                op("gpsimd", lambda E: E.dma_start(
                    out=dbg["d_x1"], in_=ARF[:]), inc=cq, incn=16, dma=True)
                op("gpsimd", lambda E: E.dma_start(
                    out=dbg["d_xg"], in_=AR[:, 12288:12288 + NH * GT]),
                    inc=cq, incn=16, dma=True)
            W("sync", cq, cq.n)
            _finish(nc, q)
            return nc

        s_last_silu = 0
        mult_done_prev = [0, 0]
        exp_guard = [0, 0]
        p_free = [0, 0]
        pt_free = [0, 0]
        yf_sread = {}
        for g in range(NG):
            q0 = CTX + g * GT

            b1_xld = []
            for t in range(4):
                if g == 0:
                    W("sync", ve, a_done_v)
                g_wait("sync", f"x1_{t}")
                lv = op("sync", lambda E, t=t, r=q0 + t * 128: E.dma_start(
                    out=x1v(t), in_=xs_d[r:r + 128, :]),
                    inc=bx[t], incn=16, dma=True)
                b1_xld.append(lv)

            g_wait("vector", "qtab")
            b1_tab_s, b1_tab_v = theta_tables(GT, q0, csQ, snQ, QSCALE,
                                              s_wait=s_last_silu,
                                              omega_wait=s_omega)
            op("scalar", lambda E: E.activation(out=dmy[:], in_=dmy[:],
                                                func=ACT.Exp, scale=0.0))

            for t in range(4):
                emit_ln_stats(x1v(t), t, waits=((bx[t], b1_xld[t]),))
            emit_newton(4)
            for t in range(4):
                emit_norm_transpose(
                    x1v(t), t,
                    lambda ci, t=t: xgv(ci)[:, t * 128:(t + 1) * 128])
            b1_xg_v = ve.n

            for h in range(NH):
                sl, bufv, ldsem, lv = wload(
                    g, wq_d[h * 128:(h + 1) * 128, :],
                    wqc_d[h * 128:(h + 1) * 128, :])
                bank = ["psA0", "psA1"][h % 2]
                b = h % 2
                W("tensor", ldsem, lv)
                if h == 0:
                    W("tensor", ve, b1_xg_v)
                g_wait("tensor", bank)
                for ci in range(16):
                    op("tensor", lambda E, b=b, ci=ci, w=bufv: E.matmul(
                        psA[:, b * 512:(b + 1) * 512],
                        w[:, ci * 128:(ci + 1) * 128],
                        xgv(ci),
                        start=(ci == 0), stop=(ci == 15)),
                        inc=(pe if ci == 15 else None))
                pq = pe.n
                wdone(sl, pe, pq)
                W("vector", pe, pq)
                if h == 0:
                    W("vector", se, b1_tab_s)
                    W("vector", ve, b1_tab_v)
                psq = psA[:, b * 512:(b + 1) * 512]
                op("vector", lambda E, p=psq: E.tensor_copy(
                    out=rtmp[0:64, :], in_=p[64:128, :]))
                op("vector", lambda E, p=psq: E.tensor_copy(
                    out=rtmp[64:128, :], in_=p[0:64, :]))
                op("vector", lambda E, p=psq: E.tensor_tensor(
                    out=atmp[:], in0=p, in1=csQ[:], op=ALU.mult))
                op("vector", lambda E: E.tensor_tensor(
                    out=rtmp[:], in0=rtmp[:], in1=snQ[:], op=ALU.mult))
                op("vector", lambda E, h=h: E.tensor_tensor(
                    out=Qfv(h), in0=atmp[:], in1=rtmp[:], op=ALU.add), inc=ve)
                g_set(bank, ve, ve.n)
            b1_q_v = ve.n
            g_set("qtab", ve, b1_q_v)

            if debug and g == 0:
                W("gpsimd", ve, b1_q_v)
                op("gpsimd", lambda E: E.dma_start(
                    out=dbg["d_qf"], in_=AR[:, 16384:16384 + NH * GT]),
                    inc=cq, incn=16, dma=True)
                op("gpsimd", lambda E: E.dma_start(
                    out=dbg["d_xg"], in_=AR[:, 32768:40960]), inc=cq, incn=16, dma=True)
                dbg_barrier()

            if stop_after == "B1" and g == 0:
                W("sync", cq, cq.n)
                _finish(nc, q)
                return nc

            NIT = 4 * NH

            def b2_item(idx):
                qt, h = idx // NH, idx % NH
                qg = 4 * g + qt
                return qt, h, qg, (M0 if qg == 0 else (M1 if qg == 1 else Mg))

            def emit_S(idx):
                qt, h, qg, mask = b2_item(idx)
                sl2 = idx % 2
                psS = psB[:, sl2 * 512:sl2 * 512 + 384]
                W("tensor", ve, b1_q_v)
                g_wait("tensor", ["psB0", "psB1"][sl2])
                W("tensor", se, exp_guard[sl2])
                op("tensor", lambda E, s=psS, h=h, qt=qt, kw=128 * qg: E.matmul(
                    s, Qfv(h)[:, qt * 128:(qt + 1) * 128],
                    Kf[:, (h // 4) * TS + kw:(h // 4) * TS + kw + 384],
                    start=True, stop=True), inc=pe)
                b2_ps[idx] = pe.n
                W("vector", pe, b2_ps[idx])
                nm = stats[:, 24 + sl2:25 + sl2]
                op("vector", lambda E, s=psS, d=nm: E.reduce_max(
                    out=d, in_=s, axis=AX.X, negate=True), inc=ve)
                b2_nm[idx] = ve.n

            def emit_soft(idx):
                qt, h, qg, mask = b2_item(idx)
                sl2 = idx % 2
                psS = psB[:, sl2 * 512:sl2 * 512 + 384]
                nm = stats[:, 24 + sl2:25 + sl2]
                W("scalar", ve, b2_nm[idx])
                W("scalar", pe, p_free[sl2])
                op("scalar", lambda E, s=psS, d=Pt[sl2], b=nm: E.activation(
                    out=d[:], in_=s, func=ACT.Exp, bias=b, scale=1.0), inc=se)
                exp_guard[sl2] = se.n
                b2_exp[idx] = se.n
                W("vector", se, b2_exp[idx])
                rs = stats[:, 26 + sl2:27 + sl2]
                ri = stats[:, 28 + sl2:29 + sl2]
                op("vector", lambda E, p=Pt[sl2], m=mask, r=rs:
                    E.scalar_tensor_tensor(
                        out=p[:], in0=p[:], scalar=1.0, in1=m[:],
                        op0=ALU.mult, op1=ALU.mult, accum_out=r))
                op("vector", lambda E, r=rs, o=ri: E.reciprocal(out=o, in_=r),
                   inc=ve)
                W("vector", ve, ve.n)
                op("vector", lambda E, p=Pt[sl2], r=ri: E.tensor_scalar(
                    out=p[:], in0=p[:], scalar1=r, scalar2=None, op0=ALU.mult),
                    inc=ve)
                b2_pf[idx] = ve.n

            def emit_tail(idx):
                qt, h, qg, mask = b2_item(idx)
                sl2 = idx % 2
                tb = 1024 + sl2 * 512
                bname = f"psB{2 + sl2}"
                W("tensor", ve, b2_pf[idx])
                g_wait("tensor", bname)
                for tix in range(3):
                    op("tensor", lambda E, p=Pt[sl2], tix=tix, tb=tb: E.transpose(
                        psB[:, tb + tix * 128:tb + (tix + 1) * 128],
                        p[:, tix * 128:(tix + 1) * 128], ident[:]),
                        inc=(pe if tix == 2 else None))
                p_tr = pe.n
                p_free[sl2] = p_tr
                W("scalar", pe, p_tr)
                W("scalar", pe, pt_free[sl2])
                op("scalar", lambda E, d=PTs[sl2], tb=tb: E.copy(
                    out=d[:], in_=psB[:, tb:tb + 384]), inc=se)
                s_pte = se.n
                g_set(bname, se, s_pte)
                yb = sl2 * 512
                W("tensor", se, s_pte)
                g_wait("tensor", ["psA0", "psA1"][sl2])
                for tix in range(3):
                    kt = qg + tix
                    op("tensor", lambda E, kt=kt, h=h, d=PTs[sl2], tix=tix, yb=yb:
                        E.matmul(
                            psA[:, yb:yb + 128],
                            Vt[:, kt * KV + (h // 4) * 128:
                               kt * KV + (h // 4) * 128 + 128],
                            d[:, tix * 128:(tix + 1) * 128],
                            start=(tix == 0), stop=(tix == 2)),
                        inc=(pe if tix == 2 else None))
                p_av = pe.n
                pt_free[sl2] = p_av
                W("scalar", pe, p_av)
                op("scalar", lambda E, h=h, qt=qt, yb=yb: E.copy(
                    out=Yfv(h)[:, qt * 128:(qt + 1) * 128],
                    in_=psA[:, yb:yb + 128]), inc=se)
                g_set(["psA0", "psA1"][sl2], se, se.n)

            b2_ps = [0] * NIT
            b2_nm = [0] * NIT
            b2_exp = [0] * NIT
            b2_pf = [0] * NIT
            emit_S(0)
            for idx in range(NIT):
                if idx + 1 < NIT:
                    emit_S(idx + 1)
                emit_soft(idx)
                if idx >= 1:
                    emit_tail(idx - 1)
            emit_tail(NIT - 1)
            b2_done_s = se.n

            if debug and g == 0:
                W("gpsimd", se, b2_done_s)
                op("gpsimd", lambda E: E.dma_start(
                    out=dbg["d_yf"], in_=AR[:, 24576:24576 + NH * GT]),
                    inc=cq, incn=16, dma=True)
                dbg_barrier()

            if stop_after == "B2" and g == 0:
                W("sync", cq, cq.n)
                _finish(nc, q)
                return nc

            op("scalar", lambda E: E.activation(out=dmy[:], in_=dmy[:],
                                                func=ACT.Silu, scale=0.0))

            for co in range(4):
                pX, pXn = (psA, "psA") if co % 2 == 0 else (psB, "psB")
                for b4 in range(4):
                    g_wait("tensor", f"{pXn}{b4}")
                if co < 2:
                    W("tensor", se, b2_done_s)
                for ci in range(16):
                    sl, bufv, ldsem, lv = wload(
                        g, wo_d[co * C + ci * 128:co * C + (ci + 1) * 128, :],
                        woc_d[co * C + ci * 128:co * C + (ci + 1) * 128, :],
                        cols=512)
                    W("tensor", ldsem, lv)
                    for tk in range(4):
                        op("tensor", lambda E, pX=pX, tk=tk, ci=ci, w=bufv:
                            E.matmul(
                                pX[:, tk * 512:(tk + 1) * 512],
                                Yfv(ci)[:, tk * 128:(tk + 1) * 128],
                                w,
                                start=(ci == 0), stop=(ci == 15)),
                            inc=(pe if tk == 3 else None))
                    wdone(sl, pe, pe.n)
                p_b3 = pe.n
                W("vector", pe, p_b3)
                for tk in range(4):
                    W("vector", bx[tk], b1_xld[tk])
                    g_wait("vector", f"x1_{tk}")
                    op("vector", lambda E, pX=pX, tk=tk, co=co: E.tensor_tensor(
                        out=x1v(tk)[:, co * 512:(co + 1) * 512],
                        in0=pX[:, tk * 512:(tk + 1) * 512],
                        in1=x1v(tk)[:, co * 512:(co + 1) * 512],
                        op=ALU.add), inc=(ve if tk == 3 else None))
                for b4 in range(4):
                    g_set(f"{pXn}{b4}", ve, ve.n)
            b3_done_v = ve.n
            for t in range(4):
                guards[f"x1_{t}"] = []

            if debug and g == 0:
                W("gpsimd", ve, b3_done_v)
                op("gpsimd", lambda E: E.dma_start(
                    out=dbg["d_x1"], in_=ARF[:]), inc=cq, incn=16, dma=True)
                dbg_barrier()

            if stop_after == "B3" and g == 0:
                W("sync", cq, cq.n)
                _finish(nc, q)
                return nc

            for t in range(4):
                emit_ln_stats(x1v(t), t,
                              waits=(((ve, b3_done_v),) if t == 0 else ()))
            emit_newton(4)
            for t in range(4):
                emit_norm_transpose(
                    x1v(t), t,
                    lambda ci, t=t: xgv(ci)[:, t * 128:(t + 1) * 128])
            ln2_v = ve.n

            mult_done = mult_done_prev
            for j in range(NJ):
                for part in range(2):
                    jj = j if part == 0 else NJ + j
                    bname = f"psA{part * 2 + (j % 2)}"
                    b = part * 2 + (j % 2)
                    sl, bufv, ldsem, lv = wload(
                        g, wgu_d[jj * 128:(jj + 1) * 128, :],
                        wguc_d[jj * 128:(jj + 1) * 128, :])
                    W("tensor", ldsem, lv)
                    if j == 0 and part == 0:
                        W("tensor", ve, ln2_v)
                        W("tensor", se, b2_done_s)
                    g_wait("tensor", bname)
                    for ci in range(16):
                        op("tensor", lambda E, b=b, ci=ci, w=bufv: E.matmul(
                            psA[:, b * 512:(b + 1) * 512],
                            w[:, ci * 128:(ci + 1) * 128],
                            xgv(ci),
                            start=(ci == 0), stop=(ci == 15)),
                            inc=(pe if ci == 15 else None))
                    wdone(sl, pe, pe.n)
                pgu = pe.n
                gb = j % 2
                ub = 2 + (j % 2)
                W("scalar", pe, pgu)
                W("scalar", ve, mult_done[j % 2])
                op("scalar", lambda E, gb=gb, j=j: E.activation(
                    out=silu_t[j % 2][:], in_=psA[:, gb * 512:(gb + 1) * 512],
                    func=ACT.Silu), inc=se)
                s_silu = se.n
                g_set(f"psA{gb}", se, s_silu)
                W("vector", se, s_silu)
                op("vector", lambda E, ub=ub, j=j: E.tensor_tensor(
                    out=hintv(j), in0=silu_t[j % 2][:],
                    in1=psA[:, ub * 512:(ub + 1) * 512], op=ALU.mult), inc=ve)
                mult_done[j % 2] = ve.n
                g_set(f"psA{ub}", ve, ve.n)
            s_last_silu = se.n
            gu_done_v = ve.n
            mult_done_prev = mult_done

            if debug and g == 0:
                W("gpsimd", ve, gu_done_v)
                op("gpsimd", lambda E: E.dma_start(
                    out=dbg["d_hint"], in_=AR[:, 0:2048]), inc=cq, incn=16, dma=True)
                dbg_barrier()

            for tp in range(2):
                for b4 in range(4):
                    g_wait("tensor", f"psA{b4}")
                    g_wait("tensor", f"psB{b4}")
                for i in range(NJ):
                    sl, bufv, ldsem, lv = wload(
                        g, wd_d[i * 128:(i + 1) * 128, :],
                        wdc_d[i * 128:(i + 1) * 128, :])
                    W("tensor", ldsem, lv)
                    if i == 0:
                        W("tensor", ve, gu_done_v)
                    for tk in range(2):
                        tok = tp * 2 + tk
                        pX = psA if tk == 0 else psB
                        for hb in range(4):
                            op("tensor", lambda E, pX=pX, hb=hb, i=i, tok=tok,
                                w=bufv: E.matmul(
                                    pX[:, hb * 512:(hb + 1) * 512],
                                    hintv(i)[:, tok * 128:(tok + 1) * 128],
                                    w[:, hb * 512:(hb + 1) * 512],
                                    start=(i == 0), stop=(i == NJ - 1)),
                                inc=(pe if (tk == 1 and hb == 3) else None))
                    wdone(sl, pe, pe.n)
                p_down = pe.n
                W("vector", pe, p_down)
                for tk in range(2):
                    tok = tp * 2 + tk
                    pX = psA if tk == 0 else psB
                    for hb in range(4):
                        op("vector", lambda E, pX=pX, hb=hb, tok=tok:
                            E.tensor_tensor(
                                out=x1v(tok)[:, hb * 512:(hb + 1) * 512],
                                in0=pX[:, hb * 512:(hb + 1) * 512],
                                in1=x1v(tok)[:, hb * 512:(hb + 1) * 512],
                                op=ALU.add), inc=(ve if hb == 3 else None))
                    v_tile = ve.n
                    W("scalar", ve, v_tile)
                    r0 = g * GT + tok * 128
                    op("scalar", lambda E, t=tok, r=r0: E.dma_start(
                        out=out_d[r:r + 128, :], in_=x1v(t)), inc=so, incn=16, dma=True)
                    pXn = "psA" if tk == 0 else "psB"
                    for b4 in range(4):
                        g_set(f"{pXn}{b4}", ve, v_tile)
            for t4 in range(4):
                guards[f"x1_{t4}"] = [(so, so.n)]


        W("sync", so, so.n)
        if debug:
            W("sync", cq, cq.n)
        _finish(nc, q)
    return nc


def _pretile(wq, wk, wv, wo, wgu, wd):
    wq_pre = np.ascontiguousarray(
        wq.reshape(16, 128, 16, 128).transpose(2, 1, 0, 3).reshape(16 * 128, C))
    wk_pre = np.ascontiguousarray(
        wk.reshape(16, 128, 4, 128).transpose(1, 2, 0, 3).reshape(128, 4 * C))
    wv_pre = np.ascontiguousarray(
        wv.reshape(16, 128, 512).transpose(1, 0, 2).reshape(128, 16 * 512))
    wo_pre = np.ascontiguousarray(
        wo.reshape(16, 128, 4, 512).transpose(2, 0, 1, 3).reshape(4 * C, 512))
    wgu_pre = np.ascontiguousarray(
        wgu.reshape(128, 128, 16, 128).transpose(0, 3, 2, 1).reshape(128 * 128, C))
    wd_pre = np.ascontiguousarray(wd.T)
    return wq_pre, wk_pre, wv_pre, wo_pre, wgu_pre, wd_pre


def make_in_maps(x, wq, wk, wv, wo, wgu, wd):
    wq_pre, wk_pre, wv_pre, wo_pre, wgu_pre, wd_pre = _pretile(
        wq, wk, wv, wo, wgu, wd)
    in_maps = []
    for c in range(8):
        b, h = c >> 1, c & 1
        if h == 0:
            xs = np.concatenate([x[b, 0:CTX], x[b, 0:TQ]], axis=0)
            p0 = -float(CTX)
        else:
            xs = x[b, TQ - CTX:2 * TQ]
            p0 = float(TQ - CTX)
        in_maps.append({
            "xs": np.ascontiguousarray(xs),
            "pos0": np.full((128, 1), p0, np.float32),
            "wq_pre": wq_pre, "wk_pre": wk_pre, "wv_pre": wv_pre,
            "wo_pre": wo_pre, "wgu_pre": wgu_pre, "wd_pre": wd_pre,
        })
    return in_maps


def kernel(L0_attn_x, L0_attn_Wq, L0_attn_Wk, L0_attn_Wv, L0_attn_Wo,
           L0_ffn_w_gate_up, L0_ffn_w_down):
    x = np.asarray(L0_attn_x, dtype=np.float32)
    in_maps = make_in_maps(
        x, np.asarray(L0_attn_Wq, np.float32), np.asarray(L0_attn_Wk, np.float32),
        np.asarray(L0_attn_Wv, np.float32), np.asarray(L0_attn_Wo, np.float32),
        np.asarray(L0_ffn_w_gate_up, np.float32),
        np.asarray(L0_ffn_w_down, np.float32))
    nc = build(debug=False)
    res = run_bass_kernel_spmd(nc, in_maps, core_ids=list(range(8)))
    out = np.empty((4, 2 * TQ, C), np.float32)
    for c in range(8):
        b, h = c >> 1, c & 1
        out[b, h * TQ:(h + 1) * TQ] = res.results[c]["out"]
    return out
